# version 01
# baseline (speedup 1.0000x reference)
import numpy as np
import ml_dtypes
from contextlib import ExitStack

import concourse.bass as bass
import concourse.mybir as mybir
from concourse.bass_utils import run_bass_kernel_spmd

F32 = mybir.dt.float32
BF16 = mybir.dt.bfloat16
I32 = mybir.dt.int32
AF = mybir.ActivationFunctionType
ALU = mybir.AluOpType
AX = mybir.AxisListType

NCORES = 8
D = 4096
S = 4096
B = 2
TOK = 1024
NCH = 32


class Sem:
    def __init__(self, handle):
        self.h = handle
        self.n = 0

    def inc(self, k=1):
        self.n += k
        return self.n


class Prog:
    ENGINES = ("sync", "tensor", "vector", "scalar", "gpsimd")

    def __init__(self):
        self.steps = {e: [] for e in self.ENGINES}

    def add(self, eng, fn):
        self.steps[eng].append(fn)

    def emit(self, nc):
        with nc.Block() as blk:
            @blk.sync
            def _(e):
                for f in self.steps["sync"]:
                    f(e)

            @blk.tensor
            def _(e):
                for f in self.steps["tensor"]:
                    f(e)

            @blk.vector
            def _(e):
                for f in self.steps["vector"]:
                    f(e)

            @blk.scalar
            def _(e):
                for f in self.steps["scalar"]:
                    f(e)

            @blk.gpsimd
            def _(e):
                for f in self.steps["gpsimd"]:
                    f(e)


def _step(prog, eng, waits, build, inc=None, inc_n=1):
    waits = [(s, v) for (s, v) in waits]

    if inc is not None:
        inc.inc(inc_n)

        def f(e):
            for s, v in waits:
                e.wait_ge(s.h, v)
            build(e).then_inc(inc.h, inc_n)
    else:
        def f(e):
            for s, v in waits:
                e.wait_ge(s.h, v)
            build(e)
    prog.add(eng, f)



def build_l0():
    nc = bass.Bass("TRN2", target_bir_lowering=False)

    uflip = nc.dram_tensor("uflip", [D, TOK], F32, kind="ExternalInput")
    lam_t = nc.dram_tensor("lam_t", [2, 2048], F32, kind="ExternalInput")
    gq_e = nc.dram_tensor("gq_e", [2, 128, 2048], F32, kind="ExternalInput")
    c_t = nc.dram_tensor("c_t", [2, 128, 32], F32, kind="ExternalInput")
    vc_t = nc.dram_tensor("vc_t", [2, 64, 64], F32, kind="ExternalInput")
    ct_v = nc.dram_tensor("ct_v", [2, 64], F32, kind="ExternalInput")
    p_v = nc.dram_tensor("p_v", [2, 64], F32, kind="ExternalInput")
    q_v = nc.dram_tensor("q_v", [2, 64], F32, kind="ExternalInput")
    bp_v = nc.dram_tensor("bp_v", [64], F32, kind="ExternalInput")
    dls = nc.dram_tensor("dls", [2], F32, kind="ExternalInput")
    dft = nc.dram_tensor("dft", [3, 64, 64], F32, kind="ExternalInput")
    tw = nc.dram_tensor("tw", [2, 64, 64], F32, kind="ExternalInput")
    jflip = nc.dram_tensor("jflip", [128, 128], F32, kind="ExternalInput")
    idn64 = nc.dram_tensor("idn64", [64, 64], F32, kind="ExternalInput")
    idn128 = nc.dram_tensor("idn128", [128, 128], F32, kind="ExternalInput")
    ones_c = nc.dram_tensor("ones_c", [128, 128], F32, kind="ExternalInput")
    w_t = nc.dram_tensor("w_t", [D, D], F32, kind="ExternalInput")
    bias = nc.dram_tensor("bias", [128, 32], F32, kind="ExternalInput")
    wr_t = nc.dram_tensor("wr_t", [128, 128], F32, kind="ExternalInput")
    acat_t = nc.dram_tensor("acat_t", [128, 1024], F32, kind="ExternalInput")
    bm_t = nc.dram_tensor("bm_t", [32, D], F32, kind="ExternalInput")
    rep4 = nc.dram_tensor("rep4", [4, 32], F32, kind="ExternalInput")

    x2T_d = nc.dram_tensor("x2T", [D, TOK], F32, kind="ExternalOutput")

    vrow_d = nc.dram_tensor("vrow_d", [512], F32)
    ar_d = nc.dram_tensor("ar_d", [2, S], F32)
    kpad_d = nc.dram_tensor("kpad_d", [128 + S + 64], BF16)

    prog = Prog()
    es = ExitStack()
    with es:
        sb = lambda name, shp, dt: es.enter_context(nc.sbuf_tensor(name, shp, dt))
        ps = lambda name, shp, dt: es.enter_context(nc.psum_tensor(name, shp, dt))
        sem = lambda name: Sem(es.enter_context(nc.semaphore(name)))

        sC = sem("sC")
        sU = sem("sU")
        sIOa = sem("sIOa")
        sIOb = sem("sIOb")
        sIOc = sem("sIOc")
        sIOd = sem("sIOd")
        sIOe = sem("sIOe")
        sIOf = sem("sIOf")
        sWl = [sem("sWl0"), sem("sWl1")]
        sMM = sem("sMM")
        sV = sem("sV")
        sA = sem("sA")
        sG = sem("sG")
        sSt = sem("sSt")

        A_s = sb("A_s", [128, NCH * TOK], BF16)
        gt_s = sb("gt_s", [128, 4096], BF16)
        js_s = sb("js_s", [128, 128], BF16)
        ones_s = sb("ones_s", [128, 128], BF16)
        onesf_s = sb("onesf_s", [1, 128], F32)
        jf_s = sb("jf_s", [128, 128], BF16)
        d1b_s = sb("d1b_s", [128, 1], F32)
        s2b_s = sb("s2b_s", [128, 1], F32)
        idn128_s = sb("idn128_s", [128, 128], F32)

        pBig = ps("pBig", [128, 2048], F32)
        pSmall = ps("pSmall", [64, 2048], F32)

        kp = {}
        for nm, shp, dt in [
            ("cau", [128, 32 * 12], F32),
            ("cre", [128, 32], F32), ("cim", [128, 32], F32),
            ("ar1", [128, 32], F32), ("ar2", [128, 32], F32),
            ("ar3", [128, 32], F32), ("ar4", [128, 32], F32),
            ("vct", [64, 128], F32), ("sv", [64, 64], F32),
            ("x2re", [64, 64], F32), ("x2im", [64, 64], F32),
            ("dftre", [64, 64], F32), ("dftim", [64, 64], F32),
            ("dftimn", [64, 64], F32), ("twre", [64, 64], F32),
            ("twim", [64, 64], F32), ("bre", [64, 64], F32),
            ("bim", [64, 64], F32), ("idn64", [64, 64], F32),
            ("sc", [64, 64], F32), ("kbf", [64, 64], BF16),
            ("z", [1, 192], BF16), ("sc1", [1, 8], F32),
        ]:
            kp[nm] = sb("kp_" + nm, shp, dt)
        kp_es = ExitStack()
        kpb = lambda name, shp, dt: kp_es.enter_context(nc.sbuf_tensor(name, shp, dt))
        for nm, shp, dt in [
            ("gqre", [128, 2048], F32), ("gqim", [128, 2048], F32),
            ("dre", [128, 2048], F32), ("dim", [128, 2048], F32),
            ("rec", [128, 2048], F32), ("t1", [128, 2048], F32),
            ("t2", [128, 2048], F32), ("lamre", [1, 2048], F32),
            ("lamim", [1, 2048], F32), ("statk", [1, 4096], F32),
        ]:
            kp[nm] = kpb("kp_" + nm, shp, dt)
        kp["vb1"] = kp["gqre"]
        kp["vb2"] = kp["gqim"]

        def ldc(dst, src, cast=False):
            eng = "gpsimd" if cast else "sync"
            _step(prog, eng, [], lambda e, d=dst, s=src: e.dma_start(out=d, in_=s),
                  inc=sC, inc_n=16)

        ldc(kp["gqre"][:, :], gq_e[0])
        ldc(kp["gqim"][:, :], gq_e[1])
        ldc(kp["lamre"][:, :], bass.AP(lam_t, 0, [[2048, 1], [1, 2048]]))
        ldc(kp["lamim"][:, :], bass.AP(lam_t, 2048, [[2048, 1], [1, 2048]]))
        ldc(kp["cre"][:, :], c_t[0])
        ldc(kp["cim"][:, :], c_t[1])
        ldc(kp["vct"][:, 0:64], vc_t[0])
        ldc(kp["vct"][:, 64:128], vc_t[1])
        ldc(kp["sv"][:, 0:1], bass.AP(ct_v, 0, [[1, 64], [1, 1]]))
        ldc(kp["sv"][:, 1:2], bass.AP(ct_v, 64, [[1, 64], [1, 1]]))
        ldc(kp["sv"][:, 2:3], bass.AP(p_v, 0, [[1, 64], [1, 1]]))
        ldc(kp["sv"][:, 3:4], bass.AP(p_v, 64, [[1, 64], [1, 1]]))
        ldc(kp["sv"][:, 4:5], bass.AP(q_v, 0, [[1, 64], [1, 1]]))
        ldc(kp["sv"][:, 5:6], bass.AP(q_v, 64, [[1, 64], [1, 1]]))
        ldc(kp["sv"][:, 6:7], bass.AP(bp_v, 0, [[1, 64], [1, 1]]))
        ldc(kp["sc1"][:, 0:2], bass.AP(dls, 0, [[2, 1], [1, 2]]))
        ldc(kp["dftre"][:, :], dft[0])
        ldc(kp["dftim"][:, :], dft[1])
        ldc(kp["dftimn"][:, :], dft[2])
        ldc(kp["twre"][:, :], tw[0])
        ldc(kp["twim"][:, :], tw[1])
        ldc(kp["idn64"][:, :], idn64[:, :])
        ldc(idn128_s[:, :], idn128[:, :])
        ldc(jf_s[:, :], jflip[:, :], cast=True)
        ldc(ones_s[:, :], ones_c[:, :], cast=True)
        ldc(onesf_s[:, :], ones_c[0:1, :])
        N_C = sC.n
        LDC = [(sC, N_C)]

        for ic in range(NCH):
            _step(prog, "gpsimd", [],
                  lambda e, i=ic: e.dma_start(
                      out=A_s[:, i * TOK:(i + 1) * TOK],
                      in_=bass.AP(uflip, i * 128 * TOK, [[TOK, 128], [1, TOK]])),
                  inc=sU, inc_n=16)
        N_U = sU.n
        LDU = [(sU, N_U)]

        def vch(waits, fn):
            _step(prog, "vector", waits, fn, inc=sV)
            return sV.n

        def ach(waits, fn):
            _step(prog, "scalar", waits, fn, inc=sA)
            return sA.n

        def mch(waits, fn):
            _step(prog, "tensor", waits, fn, inc=sMM)
            return sMM.n

        a0 = ach(LDC, lambda e: e.activation(kp["sc1"][:, 2:3], kp["sc1"][:, 1:2],
                                             AF.Exp, bias=0.0, scale=-1.0))
        v0 = vch([(sA, a0)], lambda e: e.tensor_scalar_mul(
            kp["sc1"][:, 3:4], kp["sc1"][:, 2:3], 2.0))
        v1 = vch(LDC, lambda e: e.tensor_scalar_add(
            kp["sc1"][:, 4:5], kp["sc1"][:, 0:1], 1.0))
        m0 = mch([(sV, max(v0, v1))],
                 lambda e: e.matmul(pBig[:, 0:2], onesf_s[0:1, :], kp["sc1"][0:1, 3:5],
                                    start=True, stop=True))
        v2 = vch([(sMM, m0)], lambda e: e.tensor_copy(s2b_s[:, :], pBig[:, 0:1]))
        v3 = vch([(sMM, m0)], lambda e: e.tensor_copy(d1b_s[:, :], pBig[:, 1:2]))
        vJs = vch([(sV, v3)], lambda e: e.tensor_scalar_mul(js_s[:, :], jf_s[:, :],
                                                            d1b_s[:, :]))

        _step(prog, "tensor", [(sMM, m0)],
              lambda e: e.matmul(pSmall[:, 4:5], kp["vct"][:, 0:64], kp["sv"][:, 6:7],
                                 start=True, stop=True))
        mBv = mch([], lambda e: e.matmul(pSmall[:, 5:6], kp["vct"][:, 64:128],
                                         kp["sv"][:, 6:7], start=True, stop=True))
        vB = vch([(sMM, mBv)], lambda e: e.tensor_copy(kp["sv"][:, 8:10],
                                                       pSmall[:, 4:6]))
        prods = [(0, 1, 8, 9, 16), (0, 1, 2, 3, 18), (4, 5, 8, 9, 20), (4, 5, 2, 3, 22)]
        vm = vB
        for (ar_, ai_, br_, bi_, oc) in prods:
            k = oc - 16
            vch([(sV, vB)], lambda e, a=ar_, b=br_, j=k: e.tensor_mul(
                kp["sv"][:, 32 + j:33 + j], kp["sv"][:, a:a + 1], kp["sv"][:, b:b + 1]))
            vch([(sV, vB)], lambda e, a=ai_, b=bi_, j=k: e.tensor_mul(
                kp["sv"][:, 40 + j:41 + j], kp["sv"][:, a:a + 1], kp["sv"][:, b:b + 1]))
            vch([(sV, vB)], lambda e, a=ar_, b=bi_, j=k: e.tensor_mul(
                kp["sv"][:, 48 + j:49 + j], kp["sv"][:, a:a + 1], kp["sv"][:, b:b + 1]))
            vm = vch([(sV, vB)], lambda e, a=ai_, b=br_, j=k: e.tensor_mul(
                kp["sv"][:, 56 + j:57 + j], kp["sv"][:, a:a + 1], kp["sv"][:, b:b + 1]))
        vp = vm
        for (ar_, ai_, br_, bi_, oc) in prods:
            k = oc - 16
            vch([(sV, vp)], lambda e, j=k, o=oc: e.tensor_add(
                kp["sv"][:, o:o + 1], kp["sv"][:, 32 + j:33 + j],
                kp["sv"][:, 40 + j:41 + j]))
            vm = vch([(sV, vp)], lambda e, j=k, o=oc: e.tensor_sub(
                kp["sv"][:, o + 1:o + 2], kp["sv"][:, 48 + j:49 + j],
                kp["sv"][:, 56 + j:57 + j]))
        vProd = vm

        mT = mch([(sV, vProd)], lambda e: e.transpose(
            pSmall[0:8, 8:72], kp["sv"][:, 16:24], kp["idn64"][:, :]))
        vT = vch([(sMM, mT)], lambda e: e.tensor_copy(kp["sc"][0:8, 0:64],
                                                      pSmall[0:8, 8:72]))
        _step(prog, "sync", [(sV, vT)],
              lambda e: e.dma_start(out=bass.AP(vrow_d, 0, [[64, 8], [1, 64]]),
                                    in_=kp["sc"][0:8, 0:64]), inc=sIOa, inc_n=16)
        ioV = sIOa.n

        for j in range(4):
            _step(prog, "tensor", [(sMM, mT)] if j == 0 else [],
                  lambda e, jj=j: e.matmul(pBig[:, jj * 512:(jj + 1) * 512],
                                           onesf_s[0:1, :],
                                           kp["lamre"][0:1, jj * 512:(jj + 1) * 512],
                                           start=True, stop=True),
                  inc=sMM if j == 3 else None)
        mLr = sMM.n
        vLr = vch([(sMM, mLr)], lambda e: e.tensor_copy(kp["t1"][:, :], pBig[:, :]))
        for j in range(4):
            _step(prog, "tensor", [(sV, vLr)] if j == 0 else [],
                  lambda e, jj=j: e.matmul(pBig[:, jj * 512:(jj + 1) * 512],
                                           onesf_s[0:1, :],
                                           kp["lamim"][0:1, jj * 512:(jj + 1) * 512],
                                           start=True, stop=True),
                  inc=sMM if j == 3 else None)
        mLi = sMM.n
        vLi = vch([(sMM, mLi)], lambda e: e.tensor_copy(kp["t2"][:, :], pBig[:, :]))

        vd1 = vch([(sV, max(vLr, v2))], lambda e: e.scalar_tensor_tensor(
            out=kp["dre"][:, :], in0=kp["gqre"][:, :], scalar=s2b_s[:, :],
            in1=kp["t1"][:, :], op0=ALU.mult, op1=ALU.subtract))
        vd2 = vch([(sV, vLi)], lambda e: e.scalar_tensor_tensor(
            out=kp["dim"][:, :], in0=kp["gqim"][:, :], scalar=s2b_s[:, :],
            in1=kp["t2"][:, :], op0=ALU.mult, op1=ALU.subtract))
        vD = max(vd1, vd2)
        v4 = vch([(sV, vD)], lambda e: e.tensor_mul(kp["t1"][:, :], kp["dre"][:, :],
                                                    kp["dre"][:, :]))
        v5 = vch([(sV, vD)], lambda e: e.tensor_mul(kp["t2"][:, :], kp["dim"][:, :],
                                                    kp["dim"][:, :]))
        v6 = vch([(sV, max(v4, v5))], lambda e: e.tensor_add(
            kp["t1"][:, :], kp["t1"][:, :], kp["t2"][:, :]))
        vRec = vch([(sV, v6)], lambda e: e.reciprocal(kp["rec"][:, :], kp["t1"][:, :]))

        prev_bcast_read = sMM.n
        prev_vb = vRec
        for vi in range(4):
            comp_re, comp_im = 2 * vi, 2 * vi + 1
            _step(prog, "sync", [(sIOa, ioV), (sMM, prev_bcast_read)],
                  lambda e, c=comp_re: e.dma_start(
                      out=kp["statk"][0:1, 0:2048],
                      in_=bass.AP(vrow_d, c * 64, [[2048, 1], [0, 32], [1, 64]])),
                  inc=sIOb, inc_n=16)
            _step(prog, "sync", [],
                  lambda e, c=comp_im: e.dma_start(
                      out=kp["statk"][0:1, 2048:4096],
                      in_=bass.AP(vrow_d, c * 64, [[2048, 1], [0, 32], [1, 64]])),
                  inc=sIOb, inc_n=16)
            ldv = sIOb.n
            for j in range(4):
                _step(prog, "tensor", [(sIOb, ldv), (sV, prev_vb)] if j == 0 else [],
                      lambda e, jj=j: e.matmul(
                          pBig[:, jj * 512:(jj + 1) * 512], onesf_s[0:1, :],
                          kp["statk"][0:1, jj * 512:(jj + 1) * 512],
                          start=True, stop=True),
                      inc=sMM if j == 3 else None)
            mmV = sMM.n
            vb1 = vch([(sMM, mmV)], lambda e: e.tensor_copy(kp["vb1"][:, :], pBig[:, :]))
            for j in range(4):
                _step(prog, "tensor", [(sV, vb1)] if j == 0 else [],
                      lambda e, jj=j: e.matmul(
                          pBig[:, jj * 512:(jj + 1) * 512], onesf_s[0:1, :],
                          kp["statk"][0:1, 2048 + jj * 512:2048 + (jj + 1) * 512],
                          start=True, stop=True),
                      inc=sMM if j == 3 else None)
            mmV2 = sMM.n
            prev_bcast_read = mmV2
            vb2 = vch([(sMM, mmV2)], lambda e: e.tensor_copy(kp["vb2"][:, :],
                                                             pBig[:, :]))
            w1 = vch([(sV, vb2)], lambda e: e.tensor_mul(
                kp["t1"][:, :], kp["vb1"][:, :], kp["dre"][:, :]))
            w2 = vch([(sV, vb2)], lambda e: e.tensor_mul(
                kp["t2"][:, :], kp["vb2"][:, :], kp["dim"][:, :]))
            w3 = vch([(sV, max(w1, w2))], lambda e: e.tensor_add(
                kp["t1"][:, :], kp["t1"][:, :], kp["t2"][:, :]))
            w4 = vch([(sV, w3)], lambda e: e.tensor_mul(
                kp["t1"][:, :], kp["t1"][:, :], kp["rec"][:, :]))
            w5 = vch([(sV, w4)], lambda e, o=comp_re: e.reduce_sum(
                kp["cau"][:, o * 32:(o + 1) * 32],
                kp["t1"].ap().rearrange("p (o n) -> p o n", n=64), axis=AX.X))
            w6 = vch([(sV, w4)], lambda e: e.tensor_mul(
                kp["t1"][:, :], kp["vb2"][:, :], kp["dre"][:, :]))
            w7 = vch([(sV, w3)], lambda e: e.tensor_mul(
                kp["t2"][:, :], kp["vb1"][:, :], kp["dim"][:, :]))
            w8 = vch([(sV, max(w6, w7))], lambda e: e.tensor_sub(
                kp["t1"][:, :], kp["t1"][:, :], kp["t2"][:, :]))
            w9 = vch([(sV, w8)], lambda e: e.tensor_mul(
                kp["t1"][:, :], kp["t1"][:, :], kp["rec"][:, :]))
            prev_vb = vch([(sV, w9)], lambda e, o=comp_im: e.reduce_sum(
                kp["cau"][:, o * 32:(o + 1) * 32],
                kp["t1"].ap().rearrange("p (o n) -> p o n", n=64), axis=AX.X))
        vCau = prev_vb

        Ck = lambda i: kp["cau"][:, i * 32:(i + 1) * 32]
        ar1, ar2, ar3, ar4 = kp["ar1"], kp["ar2"], kp["ar3"], kp["ar4"]
        TN1 = kp["cau"][:, 8 * 32:9 * 32]
        TN2 = kp["cau"][:, 9 * 32:10 * 32]
        TN3 = kp["cau"][:, 10 * 32:11 * 32]
        TN4 = kp["cau"][:, 11 * 32:12 * 32]
        n0 = vch([(sV, vCau)], lambda e: e.tensor_scalar_add(ar1[:, :], Ck(6), 1.0))
        n1 = vch([(sV, n0)], lambda e: e.tensor_mul(ar2[:, :], ar1[:, :], ar1[:, :]))
        n2 = vch([(sV, vCau)], lambda e: e.tensor_mul(ar3[:, :], Ck(7), Ck(7)))
        n3 = vch([(sV, max(n1, n2))], lambda e: e.tensor_add(ar2[:, :], ar2[:, :],
                                                             ar3[:, :]))
        n4 = vch([(sV, n3)], lambda e: e.reciprocal(ar2[:, :], ar2[:, :]))
        n5 = vch([(sV, vCau)], lambda e: e.tensor_mul(TN1, Ck(2), Ck(4)))
        n6 = vch([(sV, vCau)], lambda e: e.tensor_mul(TN2, Ck(3), Ck(5)))
        n7 = vch([(sV, max(n5, n6))], lambda e: e.tensor_sub(TN1, TN1, TN2))
        n8 = vch([(sV, n7)], lambda e: e.tensor_mul(TN2, Ck(2), Ck(5)))
        n9 = vch([(sV, vCau)], lambda e: e.tensor_mul(TN3, Ck(3), Ck(4)))
        n10 = vch([(sV, max(n8, n9))], lambda e: e.tensor_add(TN2, TN2, TN3))
        n11 = vch([(sV, max(n7, n10, n4))], lambda e: e.tensor_mul(TN3, TN1, ar1[:, :]))
        n12 = vch([(sV, n10)], lambda e: e.tensor_mul(TN4, TN2, Ck(7)))
        n13 = vch([(sV, max(n11, n12))], lambda e: e.tensor_add(TN3, TN3, TN4))
        n14 = vch([(sV, n13)], lambda e: e.tensor_mul(TN3, TN3, ar2[:, :]))
        n15 = vch([(sV, n13)], lambda e: e.tensor_mul(TN4, TN2, ar1[:, :]))
        n16 = vch([(sV, n15)], lambda e: e.tensor_mul(ar1[:, :], TN1, Ck(7)))
        n17 = vch([(sV, n16)], lambda e: e.tensor_sub(TN4, TN4, ar1[:, :]))
        n18 = vch([(sV, n17)], lambda e: e.tensor_mul(TN4, TN4, ar2[:, :]))
        n19 = vch([(sV, max(n14, n18))], lambda e: e.tensor_sub(TN1, Ck(0), TN3))
        n20 = vch([(sV, n18)], lambda e: e.tensor_sub(TN2, Ck(1), TN4))
        n21 = vch([(sV, n19)], lambda e: e.tensor_mul(ar1[:, :], kp["cre"][:, :], TN1))
        n22 = vch([(sV, n20)], lambda e: e.tensor_mul(ar2[:, :], kp["cim"][:, :], TN2))
        n23 = vch([(sV, max(n21, n22))], lambda e: e.tensor_sub(ar1[:, :], ar1[:, :],
                                                                ar2[:, :]))
        n24 = vch([(sV, n20)], lambda e: e.tensor_mul(ar3[:, :], kp["cre"][:, :], TN2))
        n25 = vch([(sV, max(n23, n19))], lambda e: e.tensor_mul(ar4[:, :],
                                                                kp["cim"][:, :], TN1))
        n26 = vch([(sV, max(n24, n25))], lambda e: e.tensor_add(ar3[:, :], ar3[:, :],
                                                                ar4[:, :]))
        vAr = n26

        _step(prog, "tensor", [(sV, vAr)],
              lambda e: e.transpose(pSmall[0:32, 128:256], ar1[:, :],
                                    idn128_s[:, :]))
        mTr = mch([], lambda e: e.transpose(pSmall[0:32, 256:384], ar3[:, :],
                                            idn128_s[:, :]))
        vTr = vch([(sMM, mTr)], lambda e: e.tensor_copy(
            kp["cau"][0:32, 0:256], pSmall[0:32, 128:384]))
        _step(prog, "sync", [(sV, vTr)],
              lambda e: e.dma_start(out=bass.AP(ar_d, 0, [[128, 32], [1, 128]]),
                                    in_=kp["cau"][0:32, 0:128]), inc=sIOc, inc_n=16)
        _step(prog, "sync", [],
              lambda e: e.dma_start(out=bass.AP(ar_d, S, [[128, 32], [1, 128]]),
                                    in_=kp["cau"][0:32, 128:256]), inc=sIOc, inc_n=16)
        ioAr = sIOc.n
        _step(prog, "sync", [(sIOc, ioAr)],
              lambda e: e.dma_start(out=kp["x2re"][:, :],
                                    in_=bass.AP(ar_d, 0, [[64, 64], [1, 64]])),
              inc=sIOd, inc_n=16)
        _step(prog, "sync", [],
              lambda e: e.dma_start(out=kp["x2im"][:, :],
                                    in_=bass.AP(ar_d, S, [[64, 64], [1, 64]])),
              inc=sIOd, inc_n=16)
        ioX2 = sIOd.n
        _step(prog, "tensor", [(sIOd, ioX2)],
              lambda e: e.matmul(pSmall[:, 0:64], kp["x2re"][:, :], kp["dftre"][:, :],
                                 start=True, stop=False))
        _step(prog, "tensor", [],
              lambda e: e.matmul(pSmall[:, 0:64], kp["x2im"][:, :], kp["dftimn"][:, :],
                                 start=False, stop=True))
        _step(prog, "tensor", [],
              lambda e: e.matmul(pSmall[:, 64:128], kp["x2re"][:, :], kp["dftim"][:, :],
                                 start=True, stop=False))
        mA = mch([], lambda e: e.matmul(pSmall[:, 64:128], kp["x2im"][:, :],
                                        kp["dftre"][:, :], start=False, stop=True))
        vA1 = vch([(sMM, mA)], lambda e: e.tensor_copy(kp["x2re"][:, :],
                                                       pSmall[:, 0:64]))
        vA2 = vch([(sMM, mA)], lambda e: e.tensor_copy(kp["x2im"][:, :],
                                                       pSmall[:, 64:128]))
        vA_ = max(vA1, vA2)
        q0 = vch([(sV, vA_)], lambda e: e.tensor_mul(kp["bre"][:, :], kp["x2re"][:, :],
                                                     kp["twre"][:, :]))
        q1 = vch([(sV, vA_)], lambda e: e.tensor_mul(kp["bim"][:, :], kp["x2im"][:, :],
                                                     kp["twre"][:, :]))
        q2 = vch([(sV, vA_)], lambda e: e.tensor_mul(kp["sc"][:, :], kp["x2im"][:, :],
                                                     kp["twim"][:, :]))
        q3 = vch([(sV, max(q0, q2))], lambda e: e.tensor_sub(
            kp["bre"][:, :], kp["bre"][:, :], kp["sc"][:, :]))
        q4 = vch([(sV, max(q1, q3))], lambda e: e.tensor_mul(
            kp["sc"][:, :], kp["x2re"][:, :], kp["twim"][:, :]))
        q5 = vch([(sV, q4)], lambda e: e.tensor_add(kp["bim"][:, :], kp["bim"][:, :],
                                                    kp["sc"][:, :]))
        vTw = q5
        _step(prog, "tensor", [(sV, vTw)],
              lambda e: e.matmul(pSmall[:, 0:64], kp["dftre"][:, :], kp["bre"][:, :],
                                 start=True, stop=False))
        mK = mch([], lambda e: e.matmul(pSmall[:, 0:64], kp["dftimn"][:, :],
                                        kp["bim"][:, :], start=False, stop=True))
        vKbf = vch([(sMM, mK)], lambda e: e.tensor_scalar_mul(
            kp["kbf"][:, :], pSmall[:, 0:64], 1.0 / 4096.0))
        _step(prog, "gpsimd", [], lambda e: e.memset(kp["z"][:, :], 0.0), inc=sG)
        gZ = sG.n
        _step(prog, "sync", [(sG, gZ)],
              lambda e: e.dma_start(out=kpad_d[0:128], in_=kp["z"][0:1, 0:128]),
              inc=sIOe, inc_n=16)
        _step(prog, "sync", [],
              lambda e: e.dma_start(out=kpad_d[128 + S:128 + S + 64],
                                    in_=kp["z"][0:1, 128:192]), inc=sIOe, inc_n=16)
        _step(prog, "sync", [(sV, vKbf)],
              lambda e: e.dma_start(out=bass.AP(kpad_d, 128, [[64, 64], [1, 64]]),
                                    in_=kp["kbf"][:, :]), inc=sIOe, inc_n=16)
        ioK = sIOe.n
        _step(prog, "sync", [(sIOe, ioK)],
              lambda e: e.dma_start(out=gt_s[:, :],
                                    in_=bass.AP(kpad_d, 1, [[1, 128], [128, 32],
                                                            [1, 128]])),
              inc=sIOf, inc_n=16)
        ioG = sIOf.n

        kp_es.close()
        B_s = sb("B_s", [128, NCH * TOK], BF16)
        stat_s = sb("stat_s", [1, 3072], F32)
        rstd_s = sb("rstd_s", [128, 1024], F32)
        nmr_s = sb("nmr_s", [128, 1024], F32)
        tmp1_s = sb("tmp1_s", [128, 1024], F32)
        sq_s = sb("sq_s", [128, 1024], BF16)
        wr_s = sb("wr_s", [128, 128], BF16)
        acat_s = sb("acat_s", [128, 1024], BF16)
        bm_s = sb("bm_s", [32, D], BF16)
        bias_ts = sb("bias_ts", [128, 32], F32)
        rep4_s = sb("rep4_s", [4, 32], BF16)
        expg_s = sb("expg_s", [4, 1024], BF16)
        rg_s = sb("rg_s", [1, 1024], BF16)
        p2_s = sb("p2_s", [32, 1024], BF16)
        p1_s = sb("p1_s", [32, 1024], F32)
        wslab_s = [sb(f"wslab{i}", [128, 4096], BF16) for i in range(2)]

        sC2 = sem("sC2")
        KPE = [(sV, vKbf)]

        def ldc2(dst, src_, cast=False):
            eng = "gpsimd" if cast else "sync"
            _step(prog, eng, KPE,
                  lambda e, d=dst, s=src_: e.dma_start(out=d, in_=s),
                  inc=sC2, inc_n=16)

        ldc2(wr_s[:, :], wr_t[:, :], cast=True)
        ldc2(acat_s[:, :], acat_t[:, :], cast=True)
        ldc2(bm_s[:, :], bm_t[:, :], cast=True)
        ldc2(bias_ts[:, :], bias[:, :])
        ldc2(rep4_s[:, :], rep4[:, :], cast=True)
        N_C2 = sC2.n
        LC2 = [(sC2, N_C2)]

        evac_done = [0, 0]
        for tc in range(NCH):
            reg = tc % 2
            base = reg * 1024
            for half in range(2):
                pd = pBig[:, base + half * 512: base + (half + 1) * 512]
                waits = [(sIOf, ioG), (sV, vJs)] + LDU
                if tc >= 2:
                    waits = waits + [(sV, evac_done[reg])]
                for tauc in range(tc + 1):
                    dd = tc - tauc
                    _step(prog, "tensor", waits if tauc == 0 else [],
                          lambda e, d_=dd, t_=tauc, h_=half, pp=pd: e.matmul(
                              pp, gt_s[:, d_ * 128:(d_ + 1) * 128],
                              A_s[:, t_ * TOK + h_ * 512: t_ * TOK + (h_ + 1) * 512],
                              start=(t_ == 0), stop=False))
                mmC = mch([], lambda e, t_=tc, h_=half, pp=pd: e.matmul(
                    pp, js_s[:, :],
                    A_s[:, t_ * TOK + h_ * 512: t_ * TOK + (h_ + 1) * 512],
                    start=False, stop=True))
                vch([(sMM, mmC)], lambda e, t_=tc, h_=half, pp=pd: e.tensor_copy(
                    B_s[:, t_ * TOK + h_ * 512: t_ * TOK + (h_ + 1) * 512], pp))
            evac_done[reg] = sV.n
        conv_done = sV.n

        def emit_ln(src_s, start_v, mean_off, sq_off, apply_cb):
            sq_read = sMM.n
            for ic in range(NCH):
                aw = [(sMM, sq_read)]
                if ic == 0:
                    aw.append((sV, start_v))
                aq = ach(aw, lambda e, i=ic: e.activation(
                    sq_s[:, :], src_s[:, i * TOK:(i + 1) * TOK], AF.Square,
                    bias=0.0, scale=1.0))
                for half in range(2):
                    _step(prog, "tensor",
                          [(sA, aq), (sV, start_v)] if half == 0 else [],
                          lambda e, i=ic, h=half: e.matmul(
                              pSmall[0:1, mean_off + h * 512: mean_off + (h + 1) * 512],
                              ones_s[:, 0:1],
                              src_s[:, i * TOK + h * 512: i * TOK + (h + 1) * 512],
                              start=(i == 0), stop=(i == NCH - 1)))
                    _step(prog, "tensor", [],
                          lambda e, i=ic, h=half: e.matmul(
                              pSmall[0:1, sq_off + h * 512: sq_off + (h + 1) * 512],
                              ones_s[:, 0:1], sq_s[:, h * 512:(h + 1) * 512],
                              start=(i == 0), stop=(i == NCH - 1)),
                          inc=sMM if half == 1 else None)
                sq_read = sMM.n
            mmSt = sMM.n
            c0 = vch([(sMM, mmSt)], lambda e: e.tensor_scalar_mul(
                stat_s[0:1, 0:1024], pSmall[0:1, mean_off:mean_off + 1024], 1.0 / D))
            c1 = vch([(sMM, mmSt)], lambda e: e.tensor_scalar_mul(
                stat_s[0:1, 1024:2048], pSmall[0:1, sq_off:sq_off + 1024], 1.0 / D))
            c2 = vch([(sV, c0)], lambda e: e.tensor_mul(
                stat_s[0:1, 2048:3072], stat_s[0:1, 0:1024], stat_s[0:1, 0:1024]))
            c3 = vch([(sV, max(c1, c2))], lambda e: e.tensor_sub(
                stat_s[0:1, 1024:2048], stat_s[0:1, 1024:2048],
                stat_s[0:1, 2048:3072]))
            c3b = vch([(sV, c3)], lambda e: e.tensor_scalar_add(
                stat_s[0:1, 1024:2048], stat_s[0:1, 1024:2048], 1e-5))
            aSq = ach([(sV, c3b)], lambda e: e.activation(
                stat_s[0:1, 2048:3072], stat_s[0:1, 1024:2048], AF.Sqrt,
                bias=0.0, scale=1.0))
            c4 = vch([(sA, aSq)], lambda e: e.reciprocal(
                stat_s[0:1, 1024:2048], stat_s[0:1, 2048:3072]))
            c5 = vch([(sV, c4)], lambda e: e.tensor_mul(
                stat_s[0:1, 0:1024], stat_s[0:1, 0:1024], stat_s[0:1, 1024:2048]))
            c6 = vch([(sV, c5)], lambda e: e.tensor_scalar_mul(
                stat_s[0:1, 0:1024], stat_s[0:1, 0:1024], -1.0))
            for j in range(2):
                _step(prog, "tensor", [(sV, c6)] if j == 0 else [],
                      lambda e, jj=j: e.matmul(
                          pBig[:, jj * 512:(jj + 1) * 512], onesf_s[0:1, :],
                          stat_s[0:1, 1024 + jj * 512:1024 + (jj + 1) * 512],
                          start=True, stop=True))
                _step(prog, "tensor", [],
                      lambda e, jj=j: e.matmul(
                          pBig[:, 1024 + jj * 512:1024 + (jj + 1) * 512],
                          onesf_s[0:1, :], stat_s[0:1, jj * 512:(jj + 1) * 512],
                          start=True, stop=True),
                      inc=sMM if j == 1 else None)
            mmBc = sMM.n
            vch([(sMM, mmBc)], lambda e: e.tensor_copy(rstd_s[:, :], pBig[:, 0:1024]))
            b1 = vch([(sMM, mmBc)], lambda e: e.tensor_copy(nmr_s[:, :],
                                                            pBig[:, 1024:2048]))
            bc_done = b1
            for ic in range(NCH):
                apply_cb(ic, bc_done)
            return sV.n, bc_done

        def ln1_apply(ic, bc_done):
            t = vch([(sV, bc_done)], lambda e, i=ic: e.tensor_mul(
                tmp1_s[:, :], B_s[:, i * TOK:(i + 1) * TOK], rstd_s[:, :]))
            vch([(sV, t)], lambda e, i=ic: e.tensor_add(
                B_s[:, i * TOK:(i + 1) * TOK], tmp1_s[:, :], nmr_s[:, :]))

        ln1_done, _ = emit_ln(B_s, conv_done, 0, 1024, ln1_apply)

        for ic in range(NCH):
            for half in range(2):
                _step(prog, "tensor",
                      [(sV, ln1_done)] if ic == 0 and half == 0 else [],
                      lambda e, i=ic, h=half: e.matmul(
                          pSmall[0:4, h * 512:(h + 1) * 512],
                          wr_s[:, i * 4:(i + 1) * 4],
                          B_s[:, i * TOK + h * 512: i * TOK + (h + 1) * 512],
                          start=(i == 0), stop=(i == NCH - 1)),
                      inc=sMM if (ic == NCH - 1 and half == 1) else None)
        mmG = sMM.n
        aExp = ach([(sMM, mmG)], lambda e: e.activation(
            expg_s[:, :], pSmall[0:4, 0:1024], AF.Exp, bias=0.0, scale=1.0))
        for half in range(2):
            _step(prog, "tensor", [(sA, aExp)] if half == 0 else [],
                  lambda e, h=half: e.matmul(
                      pSmall[0:1, 1024 + h * 512:1024 + (h + 1) * 512],
                      ones_s[0:4, 0:1], expg_s[0:4, h * 512:(h + 1) * 512],
                      start=True, stop=True),
                  inc=sMM if half == 1 else None)
        mmGs = sMM.n
        gr = vch([(sMM, mmGs)], lambda e: e.reciprocal(
            stat_s[0:1, 0:1024], pSmall[0:1, 1024:2048]))
        gr2 = vch([(sV, gr)], lambda e: e.tensor_copy(
            rg_s[:, :], stat_s[0:1, 0:1024]))
        for ic in range(NCH):
            for half in range(2):
                _step(prog, "tensor",
                      [(sV, ln1_done)] if ic == 0 and half == 0 else [],
                      lambda e, i=ic, h=half: e.matmul(
                          pBig[0:32, h * 512:(h + 1) * 512],
                          acat_s[:, i * 32:(i + 1) * 32],
                          B_s[:, i * TOK + h * 512: i * TOK + (h + 1) * 512],
                          start=(i == 0), stop=(i == NCH - 1)),
                      inc=sMM if (ic == NCH - 1 and half == 1) else None)
        mmP = sMM.n
        pv = vch([(sMM, mmP)], lambda e: e.tensor_copy(p1_s[:, :], pBig[0:32, 0:1024]))
        for half in range(2):
            _step(prog, "tensor", [(sA, aExp)] if half == 0 else [],
                  lambda e, h=half: e.matmul(
                      pBig[0:32, 1024 + h * 512:1024 + (h + 1) * 512],
                      rep4_s[:, :], expg_s[0:4, h * 512:(h + 1) * 512],
                      start=True, stop=True))
            _step(prog, "tensor", [(sV, gr2)] if half == 0 else [],
                  lambda e, h=half: e.matmul(
                      pSmall[0:32, h * 512:(h + 1) * 512],
                      ones_s[0:1, 0:32], rg_s[0:1, h * 512:(h + 1) * 512],
                      start=True, stop=True),
                  inc=sMM if half == 1 else None)
        mmRep = sMM.n
        pv2 = vch([(sMM, mmRep), (sV, pv)], lambda e: e.tensor_mul(
            p1_s[:, :], p1_s[:, :], pBig[0:32, 1024:2048]))
        pv3 = vch([(sV, pv2)], lambda e: e.tensor_mul(
            p2_s[:, :], p1_s[:, :], pSmall[0:32, 0:1024]))
        pDone = pv3

        ffn_evac = 0
        slab_free = [0, 0]
        for og in range(32):
            sl = og % 2
            w0 = list(KPE) + ([(sMM, slab_free[sl])] if og >= 2 else [])
            _step(prog, "gpsimd", w0,
                  lambda e, o=og, s=sl: e.dma_start(
                      out=wslab_s[s][:, :],
                      in_=bass.AP(w_t, o * 128, [[D, 128], [D * 128, 32], [1, 128]])),
                  inc=sWl[sl], inc_n=16)
            ldSlab = sWl[sl].n
            for half in range(2):
                reg = sl * 2 + half
                pd = pBig[:, reg * 512:(reg + 1) * 512]
                waits = [(sWl[sl], ldSlab), (sV, pDone), (sC2, N_C2)]
                if og >= 2:
                    waits.append((sV, ffn_evac))
                for ic in range(NCH):
                    _step(prog, "tensor", waits if ic == 0 else [],
                          lambda e, i=ic, h=half, s=sl, pp=pd: e.matmul(
                              pp, wslab_s[s][:, i * 128:(i + 1) * 128],
                              B_s[:, i * TOK + h * 512: i * TOK + (h + 1) * 512],
                              start=(i == 0), stop=False))
                mmF = mch([], lambda e, O=og, h=half, pp=pd: e.matmul(
                    pp, bm_s[:, O * 128:(O + 1) * 128],
                    p2_s[:, h * 512:(h + 1) * 512],
                    start=False, stop=True))
                if half == 1:
                    slab_free[sl] = mmF
                vch([(sMM, mmF)], lambda e, O=og, h=half, pp=pd:
                    e.scalar_tensor_tensor(
                        out=A_s[:, O * TOK + h * 512: O * TOK + (h + 1) * 512],
                        in0=pp, scalar=bias_ts[:, O:O + 1],
                        in1=B_s[:, O * TOK + h * 512: O * TOK + (h + 1) * 512],
                        op0=ALU.add, op1=ALU.add))
            if og >= 1:
                ffn_evac = sV.n
        ffn_done = sV.n

        def ln2_apply(ic, bc_done):
            w = [(sV, bc_done)]
            if ic > 0:
                w.append((sSt, sSt.n))
            t = vch(w, lambda e, i=ic: e.tensor_mul(
                tmp1_s[:, :], A_s[:, i * TOK:(i + 1) * TOK], rstd_s[:, :]))
            t2 = vch([(sV, t)], lambda e: e.tensor_add(
                tmp1_s[:, :], tmp1_s[:, :], nmr_s[:, :]))
            _step(prog, "sync", [(sV, t2)],
                  lambda e, i=ic: e.dma_start(
                      out=bass.AP(x2T_d, i * 128 * TOK, [[TOK, 128], [1, TOK]]),
                      in_=tmp1_s[:, :]),
                  inc=sSt, inc_n=16)

        emit_ln(A_s, ffn_done, 1024, 0, ln2_apply)
        _step(prog, "sync", [(sSt, sSt.n)], lambda e: e.nop())

        prog.emit(nc)
    return nc



def _l0_common(ins):
    om = np.exp(-2j * np.pi * (np.arange(S) / S)).astype(np.complex64)
    gq = ((1.0 - om) / (1.0 + om)).astype(np.complex64)
    ctab = (2.0 / (1.0 + om)).astype(np.complex64)
    gq_pm = gq.reshape(32, 128).T
    gq_e = np.repeat(gq_pm[:, :, None], 64, axis=2).reshape(128, 2048)
    gq_e2 = np.ascontiguousarray(np.stack([gq_e.real, gq_e.imag])).astype(np.float32)
    c_pm = ctab.reshape(32, 128).T
    c_t = np.ascontiguousarray(np.stack([c_pm.real, c_pm.imag])).astype(np.float32)
    lam = ins["L0_attn_Lambda_ri"]
    lam_t = np.stack([np.tile(np.asarray(lam)[:, 0], 32),
                      np.tile(np.asarray(lam)[:, 1], 32)]).astype(np.float32)
    vc = np.asarray(ins["L0_attn_Vc_ri"])
    vc_t = np.ascontiguousarray(np.stack([vc[:, :, 0].T, vc[:, :, 1].T])).astype(np.float32)
    ct_v = np.ascontiguousarray(np.asarray(ins["L0_attn_Ct_ri"])[0].T).astype(np.float32)
    p_vv = np.ascontiguousarray(np.asarray(ins["L0_attn_p_ri"]).T).astype(np.float32)
    q_vv = np.ascontiguousarray(np.asarray(ins["L0_attn_q_ri"]).T).astype(np.float32)
    F = np.exp(2j * np.pi * np.outer(np.arange(64), np.arange(64)) / 64).astype(np.complex64)
    TW = np.exp(2j * np.pi * np.outer(np.arange(64), np.arange(64)) / 4096).astype(np.complex64)
    dft = np.ascontiguousarray(np.stack([F.real, F.imag, -F.imag])).astype(np.float32)
    tw = np.ascontiguousarray(np.stack([TW.real, TW.imag])).astype(np.float32)
    jfl = np.eye(128, dtype=np.float32)[::-1].copy()
    idn64 = np.eye(64, dtype=np.float32)
    idn128 = np.eye(128, dtype=np.float32)
    ones_c = np.ones((128, 128), dtype=np.float32)
    W = np.asarray(ins["L0_ffn_W"])
    w_t = np.ascontiguousarray(W.T).astype(np.float32)
    b_in = np.ascontiguousarray(np.asarray(ins["L0_ffn_b"]).reshape(32, 128).T).astype(np.float32)
    Wr = np.asarray(ins["L0_ffn_Wr"])
    wr_t = np.ascontiguousarray(
        Wr.T.reshape(32, 128, 4).transpose(1, 0, 2).reshape(128, 128)).astype(np.float32)
    A4 = np.asarray(ins["L0_ffn_A"]).reshape(32, D)
    acat_t = np.ascontiguousarray(
        A4.T.reshape(32, 128, 32).transpose(1, 0, 2).reshape(128, 1024)).astype(np.float32)
    Bm = np.asarray(ins["L0_ffn_Bm"])
    bm_t = np.ascontiguousarray(Bm.transpose(0, 2, 1).reshape(32, D)).astype(np.float32)
    rep4 = np.zeros((4, 32), dtype=np.float32)
    for e_ in range(4):
        rep4[e_, e_ * 8:(e_ + 1) * 8] = 2.0
    dls = np.array([np.asarray(ins["L0_attn_D"])[0],
                    np.asarray(ins["L0_attn_log_step"])[0]], dtype=np.float32)
    return dict(lam_t=lam_t, gq_e=gq_e2, c_t=c_t, vc_t=vc_t, ct_v=ct_v, p_v=p_vv,
                q_v=q_vv, bp_v=np.asarray(ins["L0_attn_Bp"])[:, 0].astype(np.float32),
                dls=dls, dft=dft, tw=tw, jflip=jfl, idn64=idn64, idn128=idn128,
                ones_c=ones_c,
                w_t=w_t, bias=b_in, wr_t=wr_t, acat_t=acat_t, bm_t=bm_t, rep4=rep4)


def _core_rows(i):
    return (slice(512 * i, 512 * (i + 1)), slice(512 * (7 - i), 512 * (8 - i)))


def _l0_maps(ins):
    u = np.asarray(ins["L0_attn_u"])
    common = _l0_common(ins)
    maps = []
    for i in range(NCORES):
        r0, r1 = _core_rows(i)
        ucore = np.concatenate([u[0, r0], u[1, r1]], axis=0)
        uT = np.ascontiguousarray(ucore.T)
        uflip = np.ascontiguousarray(uT.reshape(32, 128, TOK)[:, ::-1, :].reshape(D, TOK))
        m = dict(common)
        m["uflip"] = uflip
        maps.append(m)
    return maps


def run_l0(ins):
    nc = build_l0()
    maps = _l0_maps(ins)
    res = run_bass_kernel_spmd(nc, maps, core_ids=list(range(NCORES)))
    return [r["x2T"] for r in res.results]


def kernel(**inputs):
    x2_parts = run_l0(inputs)
    x2_full_T = np.ascontiguousarray(np.concatenate(x2_parts, axis=1))
    ctx_parts = run_l1attn(inputs, x2_full_T)
    ctx_full_T = np.concatenate(ctx_parts, axis=0)
    ctx_own = [np.ascontiguousarray(ctx_full_T[:, 1024 * i:1024 * (i + 1)])
               for i in range(NCORES)]
    x4_parts = run_l1rest(inputs, x2_parts, ctx_own)
    out = np.zeros((B, S, D), dtype=np.float32)
    for i in range(NCORES):
        xc = x4_parts[i].T
        r0, r1 = _core_rows(i)
        out[0, r0] = xc[:512]
        out[1, r1] = xc[512:]
    return out



HD = 128
RSQ = 1.0 / (HD ** 0.5)


def _qcols(b, qg):
    if b == 0:
        return 1024 * qg
    return 1024 * (7 - qg) + 512


def build_l1attn():
    nc = bass.Bass("TRN2", target_bir_lowering=False)

    x2f = nc.dram_tensor("x2f", [D, 8192], F32, kind="ExternalInput")
    qwq = nc.dram_tensor("qwq", [D, 512], I32, kind="ExternalInput")
    qwk = nc.dram_tensor("qwk", [D, 128], I32, kind="ExternalInput")
    qwv = nc.dram_tensor("qwv", [D, 128], I32, kind="ExternalInput")
    scq = nc.dram_tensor("scq", [32, 512], F32, kind="ExternalInput")
    sck = nc.dram_tensor("sck", [32, 128], F32, kind="ExternalInput")
    scv = nc.dram_tensor("scv", [32, 128], F32, kind="ExternalInput")
    cosd = nc.dram_tensor("cosd", [128, 8192], F32, kind="ExternalInput")
    sind = nc.dram_tensor("sind", [128, 8192], F32, kind="ExternalInput")
    rt = nc.dram_tensor("rt", [128, 128], F32, kind="ExternalInput")
    eg = nc.dram_tensor("eg", [32, 4096], F32, kind="ExternalInput")
    idnb = nc.dram_tensor("idnb", [128, 128], F32, kind="ExternalInput")
    idnf = nc.dram_tensor("idnf", [128, 128], F32, kind="ExternalInput")
    stair = nc.dram_tensor("stair", [4, 128, 512], F32, kind="ExternalInput")

    ctxT_d = nc.dram_tensor("ctxT", [512, 8192], F32, kind="ExternalOutput")

    qT_d = nc.dram_tensor("qT_d", [512, 8192], BF16)
    fq_d = nc.dram_tensor("fq_d", [2, 1024], F32)

    prog = Prog()
    es = ExitStack()
    with es:
        sb = lambda name, shp, dt: es.enter_context(nc.sbuf_tensor(name, shp, dt))
        ps = lambda name, shp, dt: es.enter_context(nc.psum_tensor(name, shp, dt))
        sem = lambda name: Sem(es.enter_context(nc.semaphore(name)))

        sC = sem("sC")
        sQW = [sem("sQW0"), sem("sQW1")]
        sX = [sem("sX0"), sem("sX1")]
        sQS = sem("sQS")
        sQL = [sem("sQL0"), sem("sQL1")]
        sFQ = [sem("sFQ0"), sem("sFQ1")]
        sSt = [sem("sSt0"), sem("sSt1")]
        sMM = sem("sMM")
        sV = sem("sV")
        sA = sem("sA")
        sG = sem("sG")

        wq_s = sb("wq_s", [128, 32 * 512], BF16)
        wk_s = sb("wk_s", [128, 32 * 128], BF16)
        wv_s = sb("wv_s", [128, 32 * 128], BF16)
        kT_s = sb("kT_s", [128, 8192], BF16)
        v_s = sb("v_s", [128, 8192], BF16)
        cos_s = sb("cos_s", [128, 8192], BF16)
        sin_s = sb("sin_s", [128, 8192], BF16)
        rt_s = sb("rt_s", [128, 128], BF16)
        idnb_s = sb("idnb_s", [128, 128], BF16)
        idnf_s = sb("idnf_s", [128, 128], F32)
        onesf_s = sb("onesf_s", [1, 128], F32)
        eg_s = sb("eg_s", [32, 4096], BF16)
        scq_s = sb("scq_s", [32, 512], BF16)
        sck_s = sb("sck_s", [32, 128], BF16)
        scv_s = sb("scv_s", [32, 128], BF16)
        stair_s = sb("stair_s", [128, 4 * 512], F32)
        qw_t = [sb(f"qw_t{i}", [128, 512], BF16) for i in range(2)]
        x2_t = [sb(f"x2_t{i}", [128, 512], BF16) for i in range(2)]
        qsb_s = sb("qsb_s", [128, 512], BF16)
        t1_s = sb("t1_s", [128, 512], BF16)
        t2_s = sb("t2_s", [128, 512], BF16)
        t3_s = sb("t3_s", [128, 512], BF16)
        vevac_s = sb("vevac_s", [128, 512], BF16)
        qslab_s = [sb(f"qslab{i}", [128, 512], BF16) for i in range(2)]
        P_s = sb("P_s", [128, 4 * 2048], BF16)
        pT_sb = sb("pT_sb", [128, 512], BF16)
        ctxab_s = sb("ctxab_s", [128, 2 * 1024], BF16)
        stats_s = sb("stats_s", [128, 32], F32)
        fqT_s = [sb(f"fqT{i}", [8, 128], F32) for i in range(2)]
        frow_s = [sb(f"frow{i}", [1, 1024], F32) for i in range(2)]
        cout_s = [sb(f"cout{i}", [128, 512], F32) for i in range(2)]
        cmb_s = sb("cmb_s", [128, 512], F32)

        pA = ps("pA", [128, 2048], F32)
        pB = ps("pB", [128, 1024], F32)
        pC = ps("pC", [128, 512], F32)
        pD = ps("pD", [128, 1024], BF16)

        def vch(waits, fn):
            _step(prog, "vector", waits, fn, inc=sV)
            return sV.n

        def ach(waits, fn):
            _step(prog, "scalar", waits, fn, inc=sA)
            return sA.n

        def mch(waits, fn):
            _step(prog, "tensor", waits, fn, inc=sMM)
            return sMM.n

        def gch(waits, fn):
            _step(prog, "gpsimd", waits, fn, inc=sG)
            return sG.n

        def ldc(dst, src_, cast=False):
            eng = "gpsimd" if cast else "sync"
            _step(prog, eng, [], lambda e, d=dst, s=src_: e.dma_start(out=d, in_=s),
                  inc=sC, inc_n=16)

        ldc(cos_s[:, :], cosd[:, :], cast=True)
        ldc(sin_s[:, :], sind[:, :], cast=True)
        ldc(rt_s[:, :], rt[:, :], cast=True)
        ldc(idnb_s[:, :], idnb[:, :], cast=True)
        ldc(idnf_s[:, :], idnf[:, :])
        ldc(eg_s[:, :], eg[:, :], cast=True)
        ldc(scq_s[:, :], scq[:, :], cast=True)
        ldc(sck_s[:, :], sck[:, :], cast=True)
        ldc(scv_s[:, :], scv[:, :], cast=True)
        for vv in range(4):
            ldc(stair_s[:, vv * 512:(vv + 1) * 512], stair[vv])
        _step(prog, "gpsimd", [], lambda e: e.memset(onesf_s[:, :], 1.0),
              inc=sC, inc_n=1)
        N_C = sC.n
        LDC = [(sC, N_C)]

        qw_cons = [0, 0]
        for proj, (qw_d, sc_sb, wdst, No) in enumerate([
                (qwq, scq_s, wq_s, 512), (qwk, sck_s, wk_s, 128),
                (qwv, scv_s, wv_s, 128)]):
            for ic in range(NCH):
                n = proj * NCH + ic
                par = n % 2
                w0 = [(sV, qw_cons[par])] if n >= 2 else []
                _step(prog, "gpsimd", w0,
                      lambda e, d=qw_d, i=ic, p=par, NN=No: e.dma_start(
                          out=qw_t[p][:, 0:NN],
                          in_=bass.AP(d, i * 128 * NN, [[NN, 128], [1, NN]])),
                      inc=sQW[par], inc_n=16)
                ldq = sQW[par].n
                m1 = mch(LDC + [(sV, sV.n)], lambda e, i=ic, s=sc_sb, NN=No: e.matmul(
                    pA[:, 0:NN], eg_s[:, i * 128:(i + 1) * 128], s[0:32, 0:NN],
                    start=True, stop=True))
                qw_cons[par] = vch([(sQW[par], ldq), (sMM, m1)],
                                   lambda e, i=ic, p=par, w=wdst, NN=No:
                                   e.scalar_tensor_tensor(
                                       out=w[:, i * NN:(i + 1) * NN],
                                       in0=qw_t[p][:, 0:NN], scalar=-8.0,
                                       in1=pA[:, 0:NN],
                                       op0=ALU.add, op1=ALU.mult))
        dq_done = sV.n

        x_cons = {}
        evac_prev_v = 0
        prev_mR = 0
        prev_g1 = 0
        prev_v2 = 0
        prev_evT = 0
        for tcol in range(16):
            mm_tile = 0
            for i in range(NCH):
                n = tcol * NCH + i
                par = n % 2
                w0 = [(sMM, x_cons[n - 2])] if n >= 2 else []
                _step(prog, "gpsimd", w0,
                      lambda e, t=tcol, ii=i, p=par: e.dma_start(
                          out=x2_t[p][:, :],
                          in_=bass.AP(x2f, ii * 128 * 8192 + t * 512,
                                      [[8192, 128], [1, 512]])),
                      inc=sX[par], inc_n=16)
                ldx = sX[par].n
                w1 = [(sX[par], ldx)]
                if i == 0:
                    w1.append((sV, dq_done))
                    if tcol >= 1:
                        w1.append((sV, evac_prev_v))
                for h in range(4):
                    _step(prog, "tensor", w1 if h == 0 else [],
                          lambda e, ii=i, hh=h, p=par: e.matmul(
                              pA[:, hh * 512:(hh + 1) * 512],
                              wq_s[:, ii * 512 + hh * 128: ii * 512 + (hh + 1) * 128],
                              x2_t[p][:, :], start=(ii == 0), stop=(ii == NCH - 1)))
                _step(prog, "tensor", [],
                      lambda e, ii=i, p=par: e.matmul(
                          pB[:, 0:512],
                          wk_s[:, ii * 128:(ii + 1) * 128],
                          x2_t[p][:, :], start=(ii == 0), stop=(ii == NCH - 1)))
                mm_tile = mch([], lambda e, ii=i, p=par: e.matmul(
                    pB[:, 512:1024],
                    wv_s[:, ii * 128:(ii + 1) * 128],
                    x2_t[p][:, :], start=(ii == 0), stop=(ii == NCH - 1)))
                x_cons[n] = mm_tile
            mmProj = mm_tile
            for h in range(4):
                w_e0 = [(sMM, mmProj if h == 0 else prev_mR),
                        (sG, prev_g1)]
                if h == 0:
                    w_e0.append((sQS, sQS.n))
                e0 = vch(w_e0, lambda e, hh=h: e.tensor_scalar_mul(
                    qsb_s[:, :], pA[:, hh * 512:(hh + 1) * 512], RSQ))
                prev_mR = mch([(sV, max(e0, prev_v2))], lambda e: e.matmul(
                    pC[:, :], rt_s[:, :], qsb_s[:, :], start=True, stop=True))
                prev_g1 = gch([(sV, e0), (sQS, sQS.n)],
                              lambda e, t=tcol: e.tensor_mul(
                                  t1_s[:, :], qsb_s[:, :],
                                  cos_s[:, t * 512:(t + 1) * 512]))
                prev_v2 = vch([(sMM, prev_mR)], lambda e, t=tcol: e.tensor_mul(
                    t2_s[:, :], pC[:, :], sin_s[:, t * 512:(t + 1) * 512]))
                g2 = gch([(sG, prev_g1), (sV, prev_v2), (sQS, sQS.n)],
                         lambda e: e.tensor_add(t3_s[:, :], t1_s[:, :], t2_s[:, :]))
                _step(prog, "sync", [(sG, g2)],
                      lambda e, hh=h, t=tcol: e.dma_start(
                          out=bass.AP(qT_d, hh * 128 * 8192 + t * 512,
                                      [[8192, 128], [1, 512]]),
                          in_=t3_s[:, :]),
                      inc=sQS, inc_n=16)
            ek = vch([(sMM, prev_mR), (sG, prev_g1)], lambda e: e.tensor_copy(
                qsb_s[:, :], pB[:, 0:512]))
            mRk = mch([(sV, max(ek, prev_v2))], lambda e: e.matmul(
                pC[:, :], rt_s[:, :], qsb_s[:, :], start=True, stop=True))
            prev_mR = mRk
            gk1 = gch([(sV, ek)], lambda e, t=tcol: e.tensor_mul(
                t1_s[:, :], qsb_s[:, :], cos_s[:, t * 512:(t + 1) * 512]))
            prev_g1 = gk1
            vk2 = vch([(sMM, mRk)], lambda e, t=tcol: e.tensor_mul(
                t2_s[:, :], pC[:, :], sin_s[:, t * 512:(t + 1) * 512]))
            prev_v2 = vk2
            gk2 = gch([(sG, gk1), (sV, vk2)], lambda e, t=tcol: e.tensor_add(
                kT_s[:, t * 512:(t + 1) * 512], t1_s[:, :], t2_s[:, :]))
            ev0 = vch([(sMM, mmProj)], lambda e: e.tensor_copy(
                vevac_s[:, :], pB[:, 512:1024]))
            mT4 = 0
            for j in range(4):
                wT = [(sV, ev0)] if j == 0 else []
                if j == 0 and tcol >= 1:
                    wT.append((sV, prev_evT))
                mT4 = mch(wT, lambda e, jj=j: e.transpose(
                    pD[:, jj * 128:(jj + 1) * 128],
                    vevac_s[:, jj * 128:(jj + 1) * 128], idnb_s[:, :]))
            prev_evT = vch([(sMM, mT4)], lambda e, t=tcol: e.tensor_copy(
                v_s[:, t * 512:(t + 1) * 512], pD[:, 0:512]))
            evac_prev_v = sV.n
        qspill_all = sQS.n
        kv_all_v = sV.n
        kv_all_g = sG.n

        P2K = 2048
        gnum = 0
        vT_prev = kv_all_v
        ctx_slot_free = [0, 0]
        qsl_cons = [0, 0]
        st_cnt = [0, 0]
        fq_store1 = [0, 0]
        fq_used = [0, 0]
        prev_vF = 0
        prev_cdone = 0
        for h in range(4):
            for b_ in range(2):
                for qg in range(8):
                    E = qg + 1
                    EA = min(E, 4)
                    EB = E - EA
                    par = gnum % 2
                    slot = gnum % 2
                    fpar = gnum % 2
                    qc = _qcols(b_, qg)
                    w0 = [(sQS, qspill_all)]
                    if gnum >= 2:
                        w0.append((sMM, qsl_cons[par]))
                    _step(prog, "sync", w0,
                          lambda e, hh=h, q0=qc, p=par: e.dma_start(
                              out=qslab_s[p][:, :],
                              in_=bass.AP(qT_d, hh * 128 * 8192 + q0,
                                          [[8192, 128], [1, 512]])),
                          inc=sQL[par], inc_n=16)
                    ldq = sQL[par].n

                    exp_done = [sA.n] * 4
                    last_avmm = 0
                    phases = [(EA, 0, 0)] + ([(EB, 4, 1)] if EB > 0 else [])
                    for (EP, c0, ph) in phases:
                        for qt in range(4):
                            wS = [(sQL[par], ldq), (sV, kv_all_v), (sG, kv_all_g)]
                            if qt > 0:
                                wS.append((sA, exp_done[qt - 1]))
                            elif ph == 1:
                                wS.append((sA, exp_done[3]))
                                wS.append((sMM, last_avmm))
                            elif gnum >= 1:
                                wS.append((sA, exp_done[3]))
                            for ci in range(EP):
                                c = c0 + ci
                                kc = _qcols(b_, c)
                                _step(prog, "tensor", wS if ci == 0 else [],
                                      lambda e, p=par, q=qt, cc=kc, j=ci: e.matmul(
                                          pA[:, j * 512:(j + 1) * 512],
                                          qslab_s[p][:, q * 128:(q + 1) * 128],
                                          kT_s[:, cc:cc + 512],
                                          start=True, stop=True),
                                      inc=sMM if ci == EP - 1 else None)
                            mS = sMM.n
                            vw = [(sMM, mS)]
                            if c0 <= E - 1 < c0 + EP:
                                ci_d = E - 1 - c0
                                vdg = vch(vw, lambda e, q=qt, j=ci_d: e.tensor_add(
                                    pA[:, j * 512:(j + 1) * 512],
                                    pA[:, j * 512:(j + 1) * 512],
                                    stair_s[:, q * 512:(q + 1) * 512]))
                                vw = [(sV, vdg)]
                            vm = vch(vw, lambda e, q=qt, EPP=EP, p2=ph: e.reduce_max(
                                stats_s[:, p2 * 4 + q: p2 * 4 + q + 1],
                                pA[:, 0:EPP * 512], axis=AX.X, negate=True))
                            exp_done[qt] = ach([(sV, vm)],
                                               lambda e, q=qt, EPP=EP, p2=ph:
                                               e.activation(
                                P_s[:, q * P2K: q * P2K + EPP * 512],
                                pA[:, 0:EPP * 512], AF.Exp,
                                bias=stats_s[:, p2 * 4 + q: p2 * 4 + q + 1],
                                scale=1.0,
                                accum_out=stats_s[:, 8 + p2 * 4 + q:
                                                  8 + p2 * 4 + q + 1]))
                        first = True
                        for ci in range(EP):
                            c = c0 + ci
                            kc = _qcols(b_, c)
                            vb0 = kc // 128
                            for j in range(4):
                                mTT = 0
                                for qt in range(4):
                                    wT = []
                                    if qt == 0:
                                        wT = [(sV, vT_prev)]
                                    if first and qt == 0:
                                        wT.append((sA, exp_done[3]))
                                        if gnum >= 2:
                                            wT.append((sV, ctx_slot_free[slot]))
                                    mTT = mch(wT, lambda e, q=qt, jj=j, cci=ci:
                                              e.transpose(
                                                  pD[:, q * 128:(q + 1) * 128],
                                                  P_s[:, q * P2K + cci * 512 +
                                                      jj * 128:
                                                      q * P2K + cci * 512 +
                                                      (jj + 1) * 128],
                                                  idnb_s[:, :]))
                                first = False
                                vT = vch([(sMM, mTT)], lambda e: e.tensor_copy(
                                    pT_sb[:, :], pD[:, 0:512]))
                                vT_prev = vT
                                last_avmm = mch([(sV, vT)],
                                                lambda e, vv=vb0, jj=j,
                                                p2=ph,
                                                st=(ci == 0 and j == 0),
                                                sp=(ci == EP - 1 and j == 3):
                                                e.matmul(
                                    pB[:, p2 * 512:(p2 + 1) * 512],
                                    v_s[:, (vv + jj) * 128:(vv + jj + 1) * 128],
                                    pT_sb[:, :], start=st, stop=sp))
                        vch([(sMM, last_avmm)], lambda e, p2=ph, sl=slot:
                            e.tensor_copy(
                                ctxab_s[:, sl * 1024 + p2 * 512:
                                        sl * 1024 + (p2 + 1) * 512],
                                pB[:, p2 * 512:(p2 + 1) * 512]))
                    ctx_evac = sV.n
                    qsl_cons[par] = last_avmm

                    ST = stats_s
                    if EB > 0:
                        f0 = vch([(sA, exp_done[3]), (sV, prev_vF)],
                                 lambda e: e.tensor_tensor(
                                     ST[:, 24:28], ST[:, 0:4], ST[:, 4:8], ALU.min))
                        f1 = vch([(sV, f0)], lambda e: e.tensor_sub(
                            ST[:, 28:32], ST[:, 24:28], ST[:, 0:4]))
                        f2 = vch([(sV, f1)], lambda e: e.tensor_sub(
                            ST[:, 24:28], ST[:, 24:28], ST[:, 4:8]))
                        a1 = ach([(sV, f2)], lambda e: e.activation(
                            ST[:, 28:32], ST[:, 28:32], AF.Exp, bias=0.0, scale=1.0))
                        a2 = ach([(sV, f2)], lambda e: e.activation(
                            ST[:, 24:28], ST[:, 24:28], AF.Exp, bias=0.0, scale=1.0))
                        f3 = vch([(sA, a2)], lambda e: e.tensor_mul(
                            ST[:, 16:20], ST[:, 8:12], ST[:, 28:32]))
                        f4 = vch([(sV, f3)], lambda e: e.tensor_mul(
                            ST[:, 20:24], ST[:, 12:16], ST[:, 24:28]))
                        f5 = vch([(sV, f4)], lambda e: e.tensor_add(
                            ST[:, 16:20], ST[:, 16:20], ST[:, 20:24]))
                        f6 = vch([(sV, f5)], lambda e: e.reciprocal(
                            ST[:, 16:20], ST[:, 16:20]))
                        f7 = vch([(sV, f6)], lambda e: e.tensor_mul(
                            ST[:, 28:32], ST[:, 28:32], ST[:, 16:20]))
                        f8 = vch([(sV, f7)], lambda e: e.tensor_mul(
                            ST[:, 24:28], ST[:, 24:28], ST[:, 16:20]))
                        fdone = f8
                        fA_off, fB_off = 4 * 128, 0
                    else:
                        fdone = vch([(sA, exp_done[3]), (sV, prev_vF)],
                                    lambda e: e.reciprocal(
                                        ST[:, 28:32], ST[:, 8:12]))
                        fA_off, fB_off = 4 * 128, None
                    mF = mch([(sV, fdone), (sV, prev_cdone)],
                             lambda e: e.transpose(
                                 pC[0:8, 0:128], ST[:, 24:32], idnf_s[:, :]))
                    prev_vF = vch([(sMM, mF), (sFQ[fpar], fq_store1[fpar])],
                                  lambda e, fp=fpar: e.tensor_copy(
                                      fqT_s[fp][:, :], pC[0:8, 0:128]))
                    _step(prog, "sync", [(sV, prev_vF)],
                          lambda e, fp=fpar: e.dma_start(
                              out=bass.AP(fq_d, fp * 1024, [[128, 8], [1, 128]]),
                              in_=fqT_s[fp][:, :]),
                          inc=sFQ[fpar], inc_n=16)
                    fq_store1[fpar] = sFQ[fpar].n
                    _step(prog, "sync", [(sFQ[fpar], fq_store1[fpar]),
                                         (sMM, fq_used[fpar])],
                          lambda e, fp=fpar: e.dma_start(
                              out=frow_s[fp][:, :],
                              in_=bass.AP(fq_d, fp * 1024, [[1024, 1], [1, 1024]])),
                          inc=sFQ[fpar], inc_n=16)
                    st2 = sFQ[fpar].n
                    mb1 = mch([(sFQ[fpar], st2)], lambda e, fp=fpar, o=fA_off:
                              e.matmul(pC[:, :], onesf_s[0:1, :],
                                       frow_s[fp][0:1, o:o + 512],
                                       start=True, stop=True))
                    wst = [(sSt[slot], st_cnt[slot])] if gnum >= 2 else []
                    cA = vch([(sMM, mb1), (sV, ctx_evac)] + wst,
                             lambda e, sl=slot: e.tensor_mul(
                                 cout_s[sl][:, :],
                                 ctxab_s[:, sl * 1024:sl * 1024 + 512], pC[:, :]))
                    if EB > 0:
                        mb2 = mch([(sV, cA)], lambda e, fp=fpar, o=fB_off:
                                  e.matmul(pC[:, :], onesf_s[0:1, :],
                                           frow_s[fp][0:1, o:o + 512],
                                           start=True, stop=True))
                        fq_used[fpar] = mb2
                        cB = vch([(sMM, mb2)], lambda e, sl=slot: e.tensor_mul(
                            cmb_s[:, :],
                            ctxab_s[:, sl * 1024 + 512:sl * 1024 + 1024], pC[:, :]))
                        cdone = vch([(sV, cB)], lambda e, sl=slot: e.tensor_add(
                            cout_s[sl][:, :], cout_s[sl][:, :], cmb_s[:, :]))
                    else:
                        fq_used[fpar] = mb1
                        cdone = cA
                    prev_cdone = cdone
                    ctx_slot_free[slot] = cdone
                    _step(prog, "sync", [(sV, cdone)],
                          lambda e, hh=h, q0=qc, sl=slot: e.dma_start(
                              out=bass.AP(ctxT_d, hh * 128 * 8192 + q0,
                                          [[8192, 128], [1, 512]]),
                              in_=cout_s[sl][:, :]),
                          inc=sSt[slot], inc_n=16)
                    st_cnt[slot] = sSt[slot].n
                    gnum += 1
        _step(prog, "sync", [(sSt[0], sSt[0].n), (sSt[1], sSt[1].n)],
              lambda e: e.nop())
        prog.emit(nc)
    return nc


def _l1attn_maps(ins, x2_full_T):
    NH, NKV = 32, 8
    pos = np.zeros(8192, dtype=np.int64)
    for r in range(NCORES):
        pos[1024 * r: 1024 * r + 512] = np.arange(512 * r, 512 * (r + 1))
        pos[1024 * r + 512: 1024 * (r + 1)] = np.arange(512 * (7 - r), 512 * (8 - r))
    inv_freq = 1.0 / (10000.0 ** (np.arange(0, HD, 2, dtype=np.float32) / HD))
    freqs = pos[:, None].astype(np.float32) * inv_freq[None, :]
    emb = np.concatenate([freqs, freqs], axis=1)
    cosd = np.ascontiguousarray(np.cos(emb).T).astype(np.float32)
    sind = np.ascontiguousarray(np.sin(emb).T).astype(np.float32)
    R = np.zeros((128, 128), dtype=np.float32)
    R[np.arange(64), np.arange(64) + 64] = -1.0
    R[np.arange(64) + 64, np.arange(64)] = 1.0
    rt = np.ascontiguousarray(R.T)
    eg = np.zeros((32, 4096), dtype=np.float32)
    for g in range(32):
        eg[g, g * 128:(g + 1) * 128] = 1.0
    idnb = np.eye(128, dtype=np.float32)
    stair = np.zeros((4, 128, 512), dtype=np.float32)
    for vv in range(4):
        for p in range(128):
            stair[vv, p, 128 * vv + p + 1:] = -1e30
    qwq_f = np.asarray(ins["L1_attn_qw_q"])
    qwk_f = np.asarray(ins["L1_attn_qw_k"])
    qwv_f = np.asarray(ins["L1_attn_qw_v"])
    scq_f = np.asarray(ins["L1_attn_scales_q"]).astype(np.float32)
    sck_f = np.asarray(ins["L1_attn_scales_k"]).astype(np.float32)
    scv_f = np.asarray(ins["L1_attn_scales_v"]).astype(np.float32)
    common = dict(x2f=x2_full_T, cosd=cosd, sind=sind, rt=rt, eg=eg, idnb=idnb,
                  idnf=idnb, stair=stair)
    maps = []
    for i in range(NCORES):
        m = dict(common)
        m["qwq"] = np.ascontiguousarray(qwq_f[:, 512 * i:512 * (i + 1)]).astype(np.int32)
        m["scq"] = np.ascontiguousarray(scq_f[:, 512 * i:512 * (i + 1)])
        m["qwk"] = np.ascontiguousarray(qwk_f[:, 128 * i:128 * (i + 1)]).astype(np.int32)
        m["sck"] = np.ascontiguousarray(sck_f[:, 128 * i:128 * (i + 1)])
        m["qwv"] = np.ascontiguousarray(qwv_f[:, 128 * i:128 * (i + 1)]).astype(np.int32)
        m["scv"] = np.ascontiguousarray(scv_f[:, 128 * i:128 * (i + 1)])
        maps.append(m)
    return maps


def run_l1attn(ins, x2_full_T):
    nc = build_l1attn()
    maps = _l1attn_maps(ins, x2_full_T)
    res = run_bass_kernel_spmd(nc, maps, core_ids=list(range(NCORES)))
    return [r["ctxT"] for r in res.results]



def build_l1rest():
    nc = bass.Bass("TRN2", target_bir_lowering=False)

    x2o = nc.dram_tensor("x2o", [D, TOK], F32, kind="ExternalInput")
    ctxo = nc.dram_tensor("ctxo", [D, TOK], F32, kind="ExternalInput")
    qwo = nc.dram_tensor("qwo", [D, D], I32, kind="ExternalInput")
    sco = nc.dram_tensor("sco", [32, D], F32, kind="ExternalInput")
    eg = nc.dram_tensor("eg", [32, 4096], F32, kind="ExternalInput")
    a1t = nc.dram_tensor("a1t", [D, 1024], F32, kind="ExternalInput")
    b1f = nc.dram_tensor("b1f", [1024, D], F32, kind="ExternalInput")
    wr1t = nc.dram_tensor("wr1t", [128, 2048], F32, kind="ExternalInput")
    ekr = nc.dram_tensor("ekr", [64, 1024], F32, kind="ExternalInput")
    idnb = nc.dram_tensor("idnb", [128, 128], F32, kind="ExternalInput")
    ones_c = nc.dram_tensor("ones_c", [128, 128], F32, kind="ExternalInput")

    x4T_d = nc.dram_tensor("x4T", [D, TOK], F32, kind="ExternalOutput")

    prog = Prog()
    es = ExitStack()
    with es:
        sb = lambda name, shp, dt: es.enter_context(nc.sbuf_tensor(name, shp, dt))
        ps = lambda name, shp, dt: es.enter_context(nc.psum_tensor(name, shp, dt))
        sem = lambda name: Sem(es.enter_context(nc.semaphore(name)))

        sC = sem("sC")
        sXl = sem("sXl")
        sCl = sem("sCl")
        sWo = [sem("sWo0"), sem("sWo1")]
        sSo = [sem("sSo0"), sem("sSo1")]
        sB1 = [sem("sB10"), sem("sB11")]
        sMM = sem("sMM")
        sV = sem("sV")
        sA = sem("sA")
        sG = sem("sG")
        sSt = sem("sSt")

        X_s = sb("X_s", [128, NCH * TOK], BF16)
        eg_s = sb("eg_s", [32, 4096], BF16)
        wr1_s = sb("wr1_s", [128, 2048], BF16)
        ekr_s = sb("ekr_s", [64, 1024], BF16)
        stat_s = sb("stat_s", [1, 2048], F32)
        rstd_s = sb("rstd_s", [128, 1024], F32)
        nmr_s = sb("nmr_s", [128, 1024], F32)
        tmp1_s = sb("tmp1_s", [128, 1024], F32)
        sq_s = sb("sq_s", [128, 1024], BF16)
        neginf_s = sb("neginf_s", [128, 64], F32)
        idnb_s = sb("idnb_s", [128, 128], BF16)
        idnf_s = sb("idnf_s", [128, 128], F32)
        ones_s = sb("ones_s", [128, 128], BF16)
        onesf_s = sb("onesf_s", [1, 128], F32)
        op_es = ExitStack()
        opb = lambda name, shp, dt: op_es.enter_context(nc.sbuf_tensor(name, shp, dt))
        C_s = opb("C_s", [128, NCH * TOK], BF16)
        slab_s = [opb(f"slab{i}", [128, 4096], BF16) for i in range(2)]
        sco_s = [opb(f"sco{i}", [32, 512], BF16) for i in range(2)]

        pBig = ps("pBig", [128, 2048], F32)
        pS2 = ps("pS2", [128, 1024], F32)
        pB16 = ps("pB16", [128, 1024], BF16)

        def vch(waits, fn):
            _step(prog, "vector", waits, fn, inc=sV)
            return sV.n

        def ach(waits, fn):
            _step(prog, "scalar", waits, fn, inc=sA)
            return sA.n

        def mch(waits, fn):
            _step(prog, "tensor", waits, fn, inc=sMM)
            return sMM.n

        def ldc(dst, src_, cast=False):
            eng = "gpsimd" if cast else "sync"
            _step(prog, eng, [], lambda e, d=dst, s=src_: e.dma_start(out=d, in_=s),
                  inc=sC, inc_n=16)

        ldc(eg_s[:, :], eg[:, :], cast=True)
        ldc(wr1_s[:, :], wr1t[:, :], cast=True)
        ldc(ekr_s[:, :], ekr[:, :], cast=True)
        ldc(idnb_s[:, :], idnb[:, :], cast=True)
        ldc(idnf_s[:, :], idnb[:, :])
        ldc(ones_s[:, :], ones_c[:, :], cast=True)
        ldc(onesf_s[:, :], ones_c[0:1, :])
        _step(prog, "gpsimd", [], lambda e: e.memset(neginf_s[:, :], -1e30),
              inc=sC, inc_n=1)
        N_C = sC.n
        LDC = [(sC, N_C)]

        for ic in range(NCH):
            _step(prog, "gpsimd", [],
                  lambda e, i=ic: e.dma_start(
                      out=X_s[:, i * TOK:(i + 1) * TOK],
                      in_=bass.AP(x2o, i * 128 * TOK, [[TOK, 128], [1, TOK]])),
                  inc=sXl, inc_n=16)
            _step(prog, "gpsimd", [],
                  lambda e, i=ic: e.dma_start(
                      out=C_s[:, i * TOK:(i + 1) * TOK],
                      in_=bass.AP(ctxo, i * 128 * TOK, [[TOK, 128], [1, TOK]])),
                  inc=sCl, inc_n=16)
        N_X = sXl.n
        N_CT = sCl.n

        evac_prev = 0
        slab_cons = [0, 0]
        sco_cons = [0, 0]
        for og in range(NCH):
            par = og % 2
            w0 = [(sMM, slab_cons[par])] if og >= 2 else []
            _step(prog, "gpsimd", w0,
                  lambda e, o=og, p=par: e.dma_start(
                      out=slab_s[p][:, :],
                      in_=bass.AP(qwo, o * 128, [[D, 128], [D * 128, 32], [1, 128]])),
                  inc=sWo[par], inc_n=16)
            ldw = sWo[par].n
            _step(prog, "gpsimd", [(sMM, sco_cons[par])] if og >= 2 else [],
                  lambda e, o=og, p=par: e.dma_start(
                      out=sco_s[p][0:32, 0:128],
                      in_=bass.AP(sco, o * 128, [[D, 32], [1, 128]])),
                  inc=sSo[par], inc_n=16)
            lds = sSo[par].n
            prev_stt = sV.n
            for ic in range(NCH):
                m1 = mch([(sSo[par], lds), (sV, prev_stt)] + (LDC if og == 0 and ic == 0 else []),
                         lambda e, i=ic, p=par: e.matmul(
                             pS2[:, 0:128], eg_s[:, i * 128:(i + 1) * 128],
                             sco_s[p][0:32, 0:128], start=True, stop=True))
                if ic == NCH - 1:
                    sco_cons[par] = m1
                prev_stt = vch([(sWo[par], ldw), (sMM, m1)],
                               lambda e, i=ic, p=par: e.scalar_tensor_tensor(
                                   out=slab_s[p][:, i * 128:(i + 1) * 128],
                                   in0=slab_s[p][:, i * 128:(i + 1) * 128],
                                   scalar=-8.0, in1=pS2[:, 0:128],
                                   op0=ALU.add, op1=ALU.mult))
            dq = prev_stt
            for half in range(2):
                reg = par * 2 + half
                pd = pBig[:, reg * 512:(reg + 1) * 512]
                waits = [(sV, dq), (sCl, N_CT)]
                if og >= 2:
                    waits.append((sV, evac_prev))
                for ic in range(NCH):
                    _step(prog, "tensor", waits if ic == 0 else [],
                          lambda e, i=ic, h=half, p=par: e.matmul(
                              pd if False else pBig[:, (p * 2 + h) * 512:
                                                    (p * 2 + h + 1) * 512],
                              slab_s[p][:, i * 128:(i + 1) * 128],
                              C_s[:, i * TOK + h * 512: i * TOK + (h + 1) * 512],
                              start=(i == 0), stop=(i == NCH - 1)),
                          inc=sMM if ic == NCH - 1 else None)
                mmO = sMM.n
                slab_cons[par] = mmO
                vch([(sMM, mmO), (sXl, N_X)],
                    lambda e, O=og, h=half, p=par: e.tensor_add(
                        X_s[:, O * TOK + h * 512: O * TOK + (h + 1) * 512],
                        pBig[:, (p * 2 + h) * 512:(p * 2 + h + 1) * 512],
                        X_s[:, O * TOK + h * 512: O * TOK + (h + 1) * 512]))
            evac_prev = sV.n
        oproj_done = sV.n

        def emit_ln(src_s, start_v, apply_cb):
            sq_read = sMM.n
            for ic in range(NCH):
                aw = [(sMM, sq_read)]
                if ic == 0:
                    aw.append((sV, start_v))
                aq = ach(aw, lambda e, i=ic: e.activation(
                    sq_s[:, :], src_s[:, i * TOK:(i + 1) * TOK], AF.Square,
                    bias=0.0, scale=1.0))
                for half in range(2):
                    _step(prog, "tensor",
                          [(sA, aq), (sV, start_v)] if half == 0 else [],
                          lambda e, i=ic, h=half: e.matmul(
                              pS2[0:1, h * 512:(h + 1) * 512],
                              ones_s[:, 0:1],
                              src_s[:, i * TOK + h * 512: i * TOK + (h + 1) * 512],
                              start=(i == 0), stop=(i == NCH - 1)))
                    _step(prog, "tensor", [],
                          lambda e, i=ic, h=half: e.matmul(
                              pBig[0:1, h * 512:(h + 1) * 512],
                              ones_s[:, 0:1], sq_s[:, h * 512:(h + 1) * 512],
                              start=(i == 0), stop=(i == NCH - 1)),
                          inc=sMM if half == 1 else None)
                sq_read = sMM.n
            mmSt = sMM.n
            c0 = vch([(sMM, mmSt)], lambda e: e.tensor_scalar_mul(
                stat_s[0:1, 0:1024], pS2[0:1, 0:1024], 1.0 / D))
            c1 = vch([(sMM, mmSt)], lambda e: e.tensor_scalar_mul(
                stat_s[0:1, 1024:2048], pBig[0:1, 0:1024], 1.0 / D))
            c2 = vch([(sV, c0)], lambda e: e.tensor_mul(
                rstd_s[0:1, 0:1024], stat_s[0:1, 0:1024], stat_s[0:1, 0:1024]))
            c3 = vch([(sV, max(c1, c2))], lambda e: e.tensor_sub(
                stat_s[0:1, 1024:2048], stat_s[0:1, 1024:2048], rstd_s[0:1, 0:1024]))
            c3b = vch([(sV, c3)], lambda e: e.tensor_scalar_add(
                stat_s[0:1, 1024:2048], stat_s[0:1, 1024:2048], 1e-5))
            aSq = ach([(sV, c3b)], lambda e: e.activation(
                rstd_s[0:1, 0:1024], stat_s[0:1, 1024:2048], AF.Sqrt,
                bias=0.0, scale=1.0))
            c4 = vch([(sA, aSq)], lambda e: e.reciprocal(
                stat_s[0:1, 1024:2048], rstd_s[0:1, 0:1024]))
            c5 = vch([(sV, c4)], lambda e: e.tensor_mul(
                stat_s[0:1, 0:1024], stat_s[0:1, 0:1024], stat_s[0:1, 1024:2048]))
            c6 = vch([(sV, c5)], lambda e: e.tensor_scalar_mul(
                stat_s[0:1, 0:1024], stat_s[0:1, 0:1024], -1.0))
            for j in range(2):
                _step(prog, "tensor", [(sV, c6)] if j == 0 else [],
                      lambda e, jj=j: e.matmul(
                          pBig[:, jj * 512:(jj + 1) * 512], onesf_s[0:1, :],
                          stat_s[0:1, 1024 + jj * 512:1024 + (jj + 1) * 512],
                          start=True, stop=True))
                _step(prog, "tensor", [],
                      lambda e, jj=j: e.matmul(
                          pBig[:, 1024 + jj * 512:1024 + (jj + 1) * 512],
                          onesf_s[0:1, :], stat_s[0:1, jj * 512:(jj + 1) * 512],
                          start=True, stop=True),
                      inc=sMM if j == 1 else None)
            mmBc = sMM.n
            vch([(sMM, mmBc)], lambda e: e.tensor_copy(rstd_s[:, :], pBig[:, 0:1024]))
            b1v = vch([(sMM, mmBc)], lambda e: e.tensor_copy(nmr_s[:, :],
                                                             pBig[:, 1024:2048]))
            bc_done = b1v
            for ic in range(NCH):
                apply_cb(ic, bc_done)
            return sV.n

        def ln3_apply(ic, bc_done):
            t = vch([(sV, bc_done)], lambda e, i=ic: e.tensor_mul(
                tmp1_s[:, :], X_s[:, i * TOK:(i + 1) * TOK], rstd_s[:, :]))
            vch([(sV, t)], lambda e, i=ic: e.tensor_add(
                X_s[:, i * TOK:(i + 1) * TOK], tmp1_s[:, :], nmr_s[:, :]))

        ln3_done = emit_ln(X_s, oproj_done, ln3_apply)

        op_es.close()
        scf_s = sb("scf_s", [64, 1024], F32)
        a1sl_s = [sb(f"a1sl{i}", [128, 4096], BF16) for i in range(2)]
        b1sl_s = [sb(f"b1sl{i}", [128, 1024], BF16) for i in range(2)]
        hP_s = sb("hP_s", [128, 8 * 1024], BF16)
        hv_s = sb("hv_s", [128, 1024], BF16)
        gT_s = sb("gT_s", [64, 1024], BF16)
        sctok_s = sb("sctok_s", [128, 64], F32)
        msk_s = sb("msk_s", [128, 64], F32)
        x6rm_s = sb("x6rm_s", [128, 64], F32)
        gex_s = sb("gex_s", [128, 64], F32)
        t8a_s = sb("t8a_s", [128, 8], F32)
        t8b_s = sb("t8b_s", [128, 8], F32)
        nm8_s = sb("nm8_s", [128, 1], F32)
        gsum_s = sb("gsum_s", [128, 1], F32)
        grec_s = sb("grec_s", [128, 1], F32)

        for ic in range(NCH):
            for half in range(2):
                _step(prog, "tensor",
                      [(sV, ln3_done)] if ic == 0 and half == 0 else [],
                      lambda e, i=ic, h=half: e.matmul(
                          pS2[0:64, h * 512:(h + 1) * 512],
                          wr1_s[:, i * 64:(i + 1) * 64],
                          X_s[:, i * TOK + h * 512: i * TOK + (h + 1) * 512],
                          start=(i == 0), stop=(i == NCH - 1)),
                      inc=sMM if (ic == NCH - 1 and half == 1) else None)
        mmR = sMM.n
        scev = vch([(sMM, mmR)], lambda e: e.tensor_copy(
            scf_s[:, :], pS2[0:64, 0:1024]))

        gprev = 0
        for tb in range(8):
            wT = [(sV, scev)]
            if tb > 0:
                wT.append((sV, gprev))
            mT = mch(wT, lambda e, t=tb: e.transpose(
                pBig[0:128, 1024:1088], scf_s[:, t * 128:(t + 1) * 128],
                idnf_s[0:64, 0:64]))
            sc0 = vch([(sMM, mT)], lambda e: e.tensor_copy(
                sctok_s[:, :], pBig[0:128, 1024:1088]))
            m8a = vch([(sV, sc0)], lambda e: e.max(t8a_s[:, :], sctok_s[:, :]))
            mr = vch([(sV, m8a)], lambda e: e.match_replace(
                x6rm_s[:, :], t8a_s[:, :], sctok_s[:, :], -1e30))
            m8b = vch([(sV, mr)], lambda e: e.max(t8b_s[:, :], x6rm_s[:, :]))
            mk = vch([(sV, m8b)], lambda e: e.tensor_single_scalar(
                msk_s[:, :], sctok_s[:, :], t8b_s[:, 7:8], ALU.is_ge))
            mk2 = vch([(sV, mk)], lambda e: e.tensor_scalar(
                out=msk_s[:, :], in0=msk_s[:, :], scalar1=-1.0, scalar2=1e30,
                op0=ALU.add, op1=ALU.mult))
            sel = vch([(sV, mk2)], lambda e: e.tensor_add(
                x6rm_s[:, :], sctok_s[:, :], msk_s[:, :]))
            ng = vch([(sV, m8a)], lambda e: e.tensor_scalar_mul(
                nm8_s[:, :], t8a_s[:, 0:1], -1.0))
            aex = ach([(sV, max(sel, ng))], lambda e: e.activation(
                gex_s[:, :], x6rm_s[:, :], AF.Exp, bias=nm8_s[:, :], scale=1.0,
                accum_out=gsum_s[:, :]))
            rg = vch([(sA, aex)], lambda e: e.reciprocal(grec_s[:, :], gsum_s[:, :]))
            gg = vch([(sV, rg)], lambda e: e.tensor_scalar(
                out=msk_s[:, :], in0=gex_s[:, :], scalar1=grec_s[:, :],
                scalar2=2.0, op0=ALU.mult, op1=ALU.mult))
            gb = vch([(sV, gg)], lambda e: e.tensor_copy(hv_s[:, 0:64], msk_s[:, :]))
            mTg = mch([(sV, gb)], lambda e: e.transpose(
                pB16[0:64, 0:128], hv_s[:, 0:64], idnb_s[:, :]))
            gprev = vch([(sMM, mTg)], lambda e, t=tb: e.tensor_copy(
                gT_s[:, t * 128:(t + 1) * 128], pB16[0:64, 0:128]))
        gates_done = gprev

        a1_cons = [0, 0]
        hp_done = 0
        for kr in range(8):
            par = kr % 2
            w0 = [(sV, ln3_done)]
            if kr >= 2:
                w0.append((sMM, a1_cons[par]))
            _step(prog, "gpsimd", w0,
                  lambda e, k=kr, p=par: e.dma_start(
                      out=a1sl_s[p][:, :],
                      in_=bass.AP(a1t, k * 128, [[1024, 128], [1024 * 128, 32],
                                                 [1, 128]])),
                  inc=sB1[par], inc_n=16)
            lda = sB1[par].n
            for half in range(2):
                for ic in range(NCH):
                    _step(prog, "tensor",
                          [(sB1[par], lda), (sV, scev)] if ic == 0 and half == 0
                          else [],
                          lambda e, i=ic, h=half, p=par: e.matmul(
                              pS2[:, h * 512:(h + 1) * 512],
                              a1sl_s[p][:, i * 128:(i + 1) * 128],
                              X_s[:, i * TOK + h * 512: i * TOK + (h + 1) * 512],
                              start=(i == 0), stop=(i == NCH - 1)),
                          inc=sMM if ic == NCH - 1 else None)
            mmH = sMM.n
            a1_cons[par] = mmH
            hev = vch([(sMM, mmH)] + ([(sV, hp_done)] if kr >= 1 else []),
                      lambda e: e.tensor_copy(hv_s[:, :], pS2[:, 0:1024]))
            for half in range(2):
                _step(prog, "tensor",
                      [(sV, gates_done), (sV, hev)] if half == 0 else [],
                      lambda e, k=kr, h=half: e.matmul(
                          pBig[:, h * 512:(h + 1) * 512],
                          ekr_s[:, k * 128:(k + 1) * 128],
                          gT_s[0:64, h * 512:(h + 1) * 512],
                          start=True, stop=True),
                      inc=sMM if half == 1 else None)
            mmG2 = sMM.n
            hp_done = vch([(sMM, mmG2), (sV, hev)],
                          lambda e, k=kr: e.tensor_mul(
                              hP_s[:, k * 1024:(k + 1) * 1024], hv_s[:, :],
                              pBig[:, 0:1024]))
        hP_done = hp_done

        evac4 = 0
        b1_cons = [0, 0]
        for og in range(NCH):
            par = og % 2
            w0 = [(sV, hP_done)]
            if og >= 2:
                w0.append((sMM, b1_cons[par]))
            _step(prog, "gpsimd", w0,
                  lambda e, o=og, p=par: e.dma_start(
                      out=b1sl_s[p][:, :],
                      in_=bass.AP(b1f, o * 128, [[D, 128], [D * 128, 8], [1, 128]])),
                  inc=sB1[par], inc_n=16)
            ldb = sB1[par].n
            for half in range(2):
                reg = par * 2 + half
                waits = [(sB1[par], ldb), (sV, hP_done)]
                if og >= 2:
                    waits.append((sV, evac4))
                for kr in range(8):
                    _step(prog, "tensor", waits if kr == 0 else [],
                          lambda e, k=kr, h=half, p=par, r=reg: e.matmul(
                              pBig[:, r * 512:(r + 1) * 512],
                              b1sl_s[p][:, k * 128:(k + 1) * 128],
                              hP_s[:, k * 1024 + h * 512: k * 1024 + (h + 1) * 512],
                              start=(k == 0), stop=(k == 7)),
                          inc=sMM if kr == 7 else None)
                mmO2 = sMM.n
                if half == 1:
                    b1_cons[par] = mmO2
                vch([(sMM, mmO2)], lambda e, O=og, h=half, r=reg: e.tensor_add(
                    X_s[:, O * TOK + h * 512: O * TOK + (h + 1) * 512],
                    pBig[:, r * 512:(r + 1) * 512],
                    X_s[:, O * TOK + h * 512: O * TOK + (h + 1) * 512]))
            evac4 = sV.n
        ffn2_done = sV.n

        def ln4_apply(ic, bc_done):
            w = [(sV, bc_done)]
            if ic > 0:
                w.append((sSt, sSt.n))
            t = vch(w, lambda e, i=ic: e.tensor_mul(
                tmp1_s[:, :], X_s[:, i * TOK:(i + 1) * TOK], rstd_s[:, :]))
            t2 = vch([(sV, t)], lambda e: e.tensor_add(
                tmp1_s[:, :], tmp1_s[:, :], nmr_s[:, :]))
            _step(prog, "sync", [(sV, t2)],
                  lambda e, i=ic: e.dma_start(
                      out=bass.AP(x4T_d, i * 128 * TOK, [[TOK, 128], [1, TOK]]),
                      in_=tmp1_s[:, :]),
                  inc=sSt, inc_n=16)

        emit_ln(X_s, ffn2_done, ln4_apply)
        _step(prog, "sync", [(sSt, sSt.n)], lambda e: e.nop())
        prog.emit(nc)
    return nc


def _l1rest_maps(ins, x2_parts, ctx_own_parts):
    qwo_f = np.asarray(ins["L1_attn_qw_o"]).astype(np.int32)
    sco_f = np.asarray(ins["L1_attn_scales_o"]).astype(np.float32)
    A1 = np.asarray(ins["L1_ffn_A"]).reshape(1024, D).astype(np.float32)
    a1t = np.ascontiguousarray(A1.T)
    B1 = np.asarray(ins["L1_ffn_B"])
    b1f = np.ascontiguousarray(B1.transpose(0, 2, 1).reshape(1024, D)).astype(np.float32)
    Wr1 = np.asarray(ins["L1_ffn_Wr"]).astype(np.float32)
    wr1t = np.ascontiguousarray(
        Wr1.T.reshape(32, 128, 64).transpose(1, 0, 2).reshape(128, 2048))
    ekr = np.zeros((64, 1024), dtype=np.float32)
    for k in range(64):
        ekr[k, k * 16:(k + 1) * 16] = 1.0
    ekr = np.zeros((64, 1024), dtype=np.float32)
    for j in range(1024):
        ekr[j // 16, j] = 1.0
    eg = np.zeros((32, 4096), dtype=np.float32)
    for g in range(32):
        eg[g, g * 128:(g + 1) * 128] = 1.0
    idnb = np.eye(128, dtype=np.float32)
    ones_c = np.ones((128, 128), dtype=np.float32)
    common = dict(qwo=qwo_f, sco=sco_f, eg=eg, a1t=a1t, b1f=b1f, wr1t=wr1t,
                  ekr=ekr, idnb=idnb, ones_c=ones_c)
    maps = []
    for i in range(NCORES):
        m = dict(common)
        m["x2o"] = np.ascontiguousarray(x2_parts[i])
        m["ctxo"] = np.ascontiguousarray(ctx_own_parts[i])
        maps.append(m)
    return maps


def run_l1rest(ins, x2_parts, ctx_own_parts):
    nc = build_l1rest()
    maps = _l1rest_maps(ins, x2_parts, ctx_own_parts)
    res = run_bass_kernel_spmd(nc, maps, core_ids=list(range(NCORES)))
    return [r["x4T"] for r in res.results]
